# version 05
# baseline (speedup 1.0000x reference)
import numpy as np
from contextlib import ExitStack
import concourse.bass as bass
import concourse.mybir as mybir
from concourse.bass_utils import run_bass_kernel_spmd

F32 = mybir.dt.float32
BF16 = mybir.dt.bfloat16
I32 = mybir.dt.int32
U16 = mybir.dt.uint16
AF = mybir.ActivationFunctionType
ALU = mybir.AluOpType
AXX = mybir.AxisListType.X

SEQ = 2048
DM = 2048
DIN = 4096
CONVD = 4224
DPROJ = 8352
NH = 32
HD = 128
DST = 64
CHUNK = 256
NCHUNK = SEQ // CHUNK
INTER = 5504
NOC_IN = DPROJ // 128 + 1
EPS = 1e-5
PI = float(np.pi)

ALL_STAGES = ("s0", "s1", "s2", "s3", "s4", "s5", "s6", "s7",
              "s8", "s9", "s10", "s11", "s12", "s13")
import os
PUMP_S3 = int(os.environ.get("PUMP_S3", 2))
PUMP_S4 = int(os.environ.get("PUMP_S4", 1))
PUMP_S5 = int(os.environ.get("PUMP_S5", 8))
PUMP_S6 = int(os.environ.get("PUMP_S6", 3))


_DSZ = {mybir.dt.float32: 4, mybir.dt.bfloat16: 2, mybir.dt.int32: 4,
        mybir.dt.uint16: 2}

ENGS = ("sync", "scalar", "vector", "tensor", "gpsimd")


class Sched:

    PB = 32
    FB = 512

    def __init__(self):
        self.steps = []
        self.cells = {}
        self.cum = {e: 0 for e in ENGS}
        self.tinfo = {}
        self.last_cast = {}

    def reg(self, name, space, fs_bytes):
        self.tinfo[name] = (space, fs_bytes)

    def _cells(self, ap):
        t = ap.tensor
        name = t.name
        esz = _DSZ[ap.dtype]
        off_b = ap.offset * esz
        space, fs = self.tinfo.get(name, ("dram", None))
        pspan_b = 0
        fspan_b = 0
        if space == "dram":
            lo = off_b
            hi = off_b
            for st, ct in ap.ap:
                hi += abs(st) * esz * (ct - 1)
            hi += esz
            return [(name, 0, fb) for fb in range(lo // self.FB, (hi - 1) // self.FB + 1)]
        p0 = off_b // fs
        f0 = off_b % fs
        for st, ct in ap.ap:
            sb = st * esz
            if sb >= fs or sb <= -fs:
                pspan_b += abs(sb) // fs * (ct - 1)
            else:
                fspan_b += abs(sb) * (ct - 1)
        p1 = p0 + pspan_b
        f1 = f0 + fspan_b + esz
        out = []
        for pg in range(p0 // self.PB, p1 // self.PB + 1):
            for fb in range(f0 // self.FB, (f1 - 1) // self.FB + 1):
                out.append((name, pg, fb))
        return out

    def op(self, eng, fn, ins=(), outs=(), inc=1, xguard=False, extra=()):
        rc = []
        wc = []
        for ap in ins:
            rc.extend(self._cells(ap))
        for ap in outs:
            wc.extend(self._cells(ap))
        if xguard:
            wc.append(("XGUARD", 0, 0))
        self._extra = tuple(extra)
        waits = {}

        def need(e2, v):
            if e2 == eng and eng == "tensor":
                return
            if v > waits.get(e2, 0):
                waits[e2] = v

        for (e2x, vx) in self._extra:
            need(e2x, vx)
        seen = set()
        for c in rc:
            if c in seen:
                continue
            seen.add(c)
            ent = self.cells.get(c)
            if ent and ent[0]:
                need(ent[0][0], ent[0][1])
        seenw = set()
        for c in wc:
            if c in seenw:
                continue
            seenw.add(c)
            ent = self.cells.get(c)
            if ent:
                if ent[0]:
                    need(ent[0][0], ent[0][1])
                for e2, v in ent[1].items():
                    need(e2, v)
        self.cum[eng] += inc
        mycum = self.cum[eng]
        for c in seenw:
            self.cells[c] = [(eng, mycum), {}]
        for c in seen:
            if c in seenw:
                continue
            ent = self.cells.setdefault(c, [None, {}])
            if ent[1].get(eng, 0) < mycum:
                ent[1][eng] = mycum
        self.steps.append((eng, fn, inc, waits))

    def dma(self, eng, out, in_, **kw):
        g = (self.tinfo.get(out.tensor.name, ("dram",))[0] == "sbuf" and
             self.tinfo.get(in_.tensor.name, ("dram",))[0] == "sbuf")
        extra = []
        if in_.dtype != out.dtype:
            key = (in_.dtype, out.dtype)
            for k2, v2 in self.last_cast.items():
                if k2 != key:
                    extra.append(v2)
            self.op(eng, lambda e: e.dma_start(out=out, in_=in_, **kw),
                    ins=[in_], outs=[out], inc=16, xguard=g, extra=extra)
            self.last_cast[key] = (eng, self.cum[eng])
            return
        self.op(eng, lambda e: e.dma_start(out=out, in_=in_, **kw),
                ins=[in_], outs=[out], inc=16, xguard=g)

    def dmat(self, out, in_):
        self.op("sync", lambda e: e.dma_start_transpose(out, in_),
                ins=[in_], outs=[out], inc=16, xguard=True)

    def emit(self, block, nc, sems):
        done = {e: {e2: 0 for e2 in ENGS} for e in ENGS}

        def mk(eng, steps_):
            def emitf(engine):
                for (fn, inc, waits) in steps_:
                    for e2, v in waits.items():
                        if done[eng][e2] < v:
                            engine.wait_ge(sems[e2], v)
                            done[eng][e2] = v
                    fn(engine).then_inc(sems[eng], inc)
                for e2 in ENGS:
                    if e2 != eng and self.cum[e2] > done[eng][e2]:
                        engine.wait_ge(sems[e2], self.cum[e2])
            return emitf

        for name in ENGS:
            steps_ = [(fn, inc, w) for (e, fn, inc, w) in self.steps if e == name]
            if steps_ or name == "sync":
                getattr(block, name)(mk(name, steps_))


def build_program(stages=ALL_STAGES, taps=()):
    nc = bass.Bass("TRN2", target_bir_lowering=False, debug=False)
    st_set = set(stages)
    tap_set = set(taps)

    def dram(name, shape, dtype, kind="Internal"):
        if kind == "Internal" and name in tap_set:
            kind = "ExternalOutput"
        return nc.dram_tensor(name, shape, dtype, kind=kind).ap()

    U = dram("u", [SEQ, DM], F32, "ExternalInput")
    W_IN = dram("w_in", [DM, DPROJ], F32, "ExternalInput")
    CONV_W = dram("conv_w", [33, 128, 4], F32, "ExternalInput")
    CONV_B = dram("conv_b", [33, 128, 1], F32, "ExternalInput")
    DT_BIAS = dram("dt_bias", [1, NH], F32, "ExternalInput")
    A_LOG = dram("a_log", [1, NH], F32, "ExternalInput")
    DD = dram("dd", [1, NH], F32, "ExternalInput")
    NORM_W = dram("norm_w", [32, 128, 1], F32, "ExternalInput")
    W_OUT = dram("w_out", [DIN, DM], F32, "ExternalInput")
    LUTV = dram("lutv", [1, 16], F32, "ExternalInput")
    GIDX = dram("gidx", [INTER, DM], I32, "ExternalInput")
    UIDX = dram("uidx", [INTER, DM], I32, "ExternalInput")
    DIDX = dram("didx", [DM, INTER], I32, "ExternalInput")
    WQ = dram("wq", [DM, DM], F32, "ExternalInput")
    WK = dram("wk", [1024, DM], F32, "ExternalInput")
    WV = dram("wv", [1024, DM], F32, "ExternalInput")
    WO = dram("wo", [DM, DM], F32, "ExternalInput")
    BQ = dram("bq", [16, 128, 1], F32, "ExternalInput")
    BK = dram("bk", [8, 128, 1], F32, "ExternalInput")
    BV = dram("bv", [8, 128, 1], F32, "ExternalInput")
    BO = dram("bo", [16, 128, 1], F32, "ExternalInput")
    RMAT = dram("rmat", [64, 128], F32, "ExternalInput")
    W_BASE = dram("w_base", [DM, DM], F32, "ExternalInput")
    B_BASE = dram("b_base", [16, 128, 1], F32, "ExternalInput")
    A_S = dram("a_s", [DM, 16], F32, "ExternalInput")
    B_S = dram("b_s", [16, DM], F32, "ExternalInput")
    A_R = dram("a_r", [6, DM, 16], F32, "ExternalInput")
    B_R = dram("b_r", [6, 16, DM], F32, "ExternalInput")
    W_ROUTER = dram("w_router", [DM, 6], F32, "ExternalInput")
    B_ROUTER = dram("b_router", [6, 1], F32, "ExternalInput")

    Y = dram("y", [SEQ, DM], F32, "ExternalOutput")

    XLNTM = dram("xlntm", [SEQ, DM], BF16)
    SILUZ = dram("siluz", [DIN, SEQ], BF16)
    XBCFM = dram("xbcfm", [CONVD, SEQ], BF16)
    XCFM = dram("xcfm", [CONVD, SEQ], BF16)
    YRMS = dram("yrms", [DIN, SEQ], BF16)
    BOUT = dram("bout", [DM, SEQ], BF16)
    X1 = dram("x1", [SEQ, DM], F32)
    X2 = dram("x2", [SEQ, DM], F32)
    X3 = dram("x3", [SEQ, DM], F32)
    IXA = dram("ixa", [INTER, DM], U16)
    IXB = dram("ixb", [DM, INTER], U16)
    GW = dram("gw", [DM, INTER], BF16)
    UW = dram("uw", [DM, INTER], BF16)
    DW = dram("dw", [INTER, DM], BF16)
    HFF = dram("hff", [INTER, SEQ], BF16)
    WQS = dram("wqs", [DM, DM], BF16)
    WKS = dram("wks", [1024, DM], BF16)
    WVS = dram("wvs", [1024, DM], BF16)
    WOS = dram("wos", [DM, DM], BF16)
    QFM = dram("qfm", [DM, SEQ], BF16)
    KFM = dram("kfm", [1024, SEQ], BF16)
    VFM = dram("vfm", [1024, SEQ], BF16)
    QPFM = dram("qpfm", [DM, SEQ], BF16)
    KPFM = dram("kpfm", [1024, SEQ], BF16)
    VSC = dram("vsc", [1024, SEQ], BF16)
    OTM = dram("otm", [SEQ, DM], BF16)

    p = Sched()

    with ExitStack() as stk:
        E = stk.enter_context

        def SB(name, shape, dtype):
            t = E(nc.sbuf_tensor(name, shape, dtype))
            p.reg(name, "sbuf", int(np.prod(shape[1:])) * _DSZ[dtype])
            return t

        def PS(name, shape, dtype):
            t = E(nc.psum_tensor(name, shape, dtype))
            p.reg(name, "sbuf", int(np.prod(shape[1:])) * _DSZ[dtype])
            return t

        XLN = SB("xlns", [128, 16 * SEQ], BF16)
        AR = SB("arena", [128, 32768], BF16)
        FA = SB("f32a", [128, 3072], F32)
        WBF = SB("wbf", [128, 4, 128], BF16)
        IOT = SB("iot", [128, 1024], I32)
        IDF = SB("idf", [128, 128], F32)
        IDB = SB("idb", [128, 128], BF16)
        ONEB = SB("oneb", [128, 128], BF16)
        TRI = SB("tri", [128, 128], F32)
        ONEF = SB("onef", [128, 128], F32)
        ONE1 = SB("one1", [1, 128], F32)
        SELH = SB("selh", [32, 256], F32)
        SEL6 = SB("sel6", [6, 96], F32)
        MSK = SB("msk", [128, 512], BF16)
        CBC = SB("cbc", [128, 33], F32)
        NWC = SB("nwc", [128, 32], F32)
        BQC = SB("bqc", [128, 16], F32)
        BKC = SB("bkc", [128, 8], F32)
        BVC = SB("bvc", [128, 8], F32)
        BOC = SB("boc", [128, 16], F32)
        BBC = SB("bbc", [128, 16], F32)
        BRC = SB("brc", [6, 1], F32)
        LUTB = SB("lutb", [128, 16], F32)
        ROWS = SB("rows", [1, 32], F32)
        ROWS2 = SB("rows2", [1, 32], F32)
        DTBC = SB("dtbc", [128, 32], F32)
        ABC = SB("abc", [128, 32], F32)
        DBC = SB("dbc", [128, 32], F32)
        DTRAW = SB("dtraw", [32, SEQ], F32)
        DTT = SB("dtt", [128, 512], F32)
        ADTT = SB("adtt", [128, 512], F32)
        LNDT = SB("lndt", [128, 512], F32)
        ACUM = SB("acum", [128, 512], F32)
        ACFM = SB("acfm", [32, 2048], F32)
        BCF = SB("bcf", [128, 2048], BF16)
        BCT = SB("bct", [128, 2048], BF16)
        RTB = SB("rtb", [128, 64], BF16)
        SHC = SB("shc", [128, 1], F32)
        STATE = SB("state", [64, 128 * 32], F32)
        STATEB = SB("stateb", [64, 256], BF16)
        CFMS = SB("cfms", [64, 2048], BF16)
        SSD1 = SB("ssd1", [128, 2048], BF16)
        P = [PS(f"ps{i}", [128, 512], F32) for i in range(8)]
        sems = {e: E(nc.semaphore("sem_" + e)) for e in ENGS}
        block = E(nc.Block())

        APT = bass.AP

        def _aps(*xs):
            return [x for x in xs if isinstance(x, APT)]

        def ts(eng, out, in0, s1, s2, op0, op1=None):
            if op1 is None:
                p.op(eng, lambda e: e.tensor_scalar(out, in0, s1, None, op0=op0),
                     ins=_aps(in0, s1), outs=[out])
            else:
                p.op(eng, lambda e: e.tensor_scalar(out, in0, s1, s2, op0=op0, op1=op1),
                     ins=_aps(in0, s1, s2), outs=[out])

        def tt(eng, out, a, b, op):
            p.op(eng, lambda e: e.tensor_tensor(out, a, b, op=op),
                 ins=[a, b], outs=[out])

        def stt(eng, out, in0, scal, in1, op0, op1):
            p.op(eng, lambda e: e.scalar_tensor_tensor(out, in0, scal, in1,
                                                       op0=op0, op1=op1),
                 ins=_aps(in0, scal, in1), outs=[out])

        def act(eng, out, in_, func, **kw):
            aux = _aps(kw.get("bias"), kw.get("scale"), kw.get("accum_out"))
            aout = _aps(kw.get("accum_out"))
            p.op(eng, lambda e: e.activation(out, in_, func, **kw),
                 ins=_aps(in_, kw.get("bias"), kw.get("scale")), outs=[out] + aout)

        def cp(eng, out, in_):
            p.op(eng, lambda e: e.tensor_copy(out, in_), ins=[in_], outs=[out])

        def mm(out, lhsT, rhs, start, stop):
            p.op("tensor", lambda e: e.matmul(out, lhsT, rhs, start=start, stop=stop),
                 ins=[lhsT, rhs, out], outs=[out])

        def tr(out, in_, ident):
            p.op("tensor", lambda e: e.transpose(out, in_, ident),
                 ins=[in_, ident], outs=[out])

        def iota(out, pattern, base, cm):
            p.op("gpsimd", lambda e: e.iota(out, pattern=pattern, base=base,
                                            channel_multiplier=cm), outs=[out])

        def memset(eng, ap_, val):
            p.op(eng, lambda e: e.memset(ap_, val), outs=[ap_])

        if "s0" in st_set:
            iota(IOT[:, 0:128], [[1, 128]], 0, -1)
            ts("vector", IDF[:, :], IOT[:, 0:128], 0.0, None, ALU.is_equal)
            cp("vector", IDB[:, :], IDF[:, :])
            ts("vector", TRI[:, :], IOT[:, 0:128], 0.0, None, ALU.is_ge)
            memset("vector", ONEB[:, :], 1.0)
            memset("vector", ONEF[:, :], 1.0)
            memset("vector", ONE1[:, :], 1.0)
            iota(IOT[0:32, 0:128], [[0, 128]], 0, 1)
            for e6 in range(6):
                ts("vector", SEL6[:, e6 * 16:(e6 + 1) * 16], IOT[0:6, 0:16],
                   float(e6), None, ALU.is_equal)
            iota(IOT[:, 0:256], [[1, 256]], 0, -1)
            ts("vector", MSK[:, 0:256], IOT[:, 0:256], 0.0, None, ALU.is_ge)
            iota(IOT[:, 0:256], [[1, 256]], -128, -1)
            ts("vector", MSK[:, 256:512], IOT[:, 0:256], 0.0, None, ALU.is_ge)
            for c in range(33):
                p.dma("sync", CBC[:, c:c + 1], CONV_B[c])
            for c in range(32):
                p.dma("sync", NWC[:, c:c + 1], NORM_W[c])
            for c in range(16):
                p.dma("sync", BQC[:, c:c + 1], BQ[c])
                p.dma("sync", BOC[:, c:c + 1], BO[c])
                p.dma("sync", BBC[:, c:c + 1], B_BASE[c])
            for c in range(8):
                p.dma("sync", BKC[:, c:c + 1], BK[c])
                p.dma("sync", BVC[:, c:c + 1], BV[c])
            p.dma("sync", BRC[:, :], B_ROUTER)
            p.dma("sync", ROWS[:, 0:16], LUTV)
            mm(P[0][:, 0:16], ONE1[:, :], ROWS[:, 0:16], True, True)
            cp("vector", LUTB[:, :], P[0][:, 0:16])
            p.dma("sync", ROWS[:, :], DT_BIAS)
            mm(P[0][:, 0:32], ONE1[:, :], ROWS[:, :], True, True)
            cp("vector", DTBC[:, :], P[0][:, 0:32])
            p.dma("sync", ROWS[:, :], A_LOG)
            act("scalar", ROWS2[:, :], ROWS[:, :], AF.Exp)
            ts("vector", ROWS2[:, :], ROWS2[:, :], -1.0, None, ALU.mult)
            mm(P[0][:, 0:32], ONE1[:, :], ROWS2[:, :], True, True)
            cp("vector", ABC[:, :], P[0][:, 0:32])
            p.dma("sync", ROWS[:, :], DD)
            mm(P[0][:, 0:32], ONE1[:, :], ROWS[:, :], True, True)
            cp("vector", DBC[:, :], P[0][:, 0:32])
            p.dma("sync", FA[0:64, 0:128], RMAT)
            tr(P[0][:, 0:64], FA[0:64, 0:128], IDF[0:64, 0:64])
            cp("vector", RTB[:, :], P[0][:, 0:64])

        def ln_tile(src_f32_tile, out_bf_tile):
            p.op("vector", lambda e: e.reduce_sum(FA[:, 2048:2049], src_f32_tile, axis=AXX),
                 ins=[src_f32_tile], outs=[FA[:, 2048:2049]])
            ts("vector", FA[:, 2048:2049], FA[:, 2048:2049], float(1.0 / DM), None, ALU.mult)
            act("scalar", AR[:, 2048:4096], src_f32_tile, AF.Square,
                accum_out=FA[:, 2049:2050])
            ts("vector", FA[:, 2049:2050], FA[:, 2049:2050], float(1.0 / DM), None, ALU.mult)
            tt("vector", FA[:, 2050:2051], FA[:, 2048:2049], FA[:, 2048:2049], ALU.mult)
            tt("vector", FA[:, 2049:2050], FA[:, 2049:2050], FA[:, 2050:2051], ALU.subtract)
            ts("vector", FA[:, 2049:2050], FA[:, 2049:2050], EPS, None, ALU.add)
            act("scalar", FA[:, 2049:2050], FA[:, 2049:2050], AF.Ln)
            act("scalar", FA[:, 2049:2050], FA[:, 2049:2050], AF.Exp, scale=-0.5)
            ts("vector", out_bf_tile, src_f32_tile, FA[:, 2048:2049],
               FA[:, 2049:2050], ALU.subtract, ALU.mult)

        if "s1" in st_set:
            for t in range(16):
                p.dma("sync", FA[:, 0:2048], U[t * 128:(t + 1) * 128, :])
                ln_tile(FA[:, 0:2048], AR[:, 0:2048])
                p.dma("sync", XLNTM[t * 128:(t + 1) * 128, :], AR[:, 0:2048])

        def load_xln_fm():
            for k in range(16):
                p.dmat(XLN[:, k * SEQ:(k + 1) * SEQ], XLNTM[:, k * 128:(k + 1) * 128])

        W_IN_R = W_IN.rearrange("(kc p) o -> p kc o", p=128)

        if "s2" in st_set:
            load_xln_fm()
            ARW_ = [AR[:, 2048:4096].rearrange("p (k o) -> p k o", k=16),
                    AR[:, 4096:6144].rearrange("p (k o) -> p k o", k=16)]
            STG_ = [AR[:, 6144:8192], AR[:, 8192:10240]]
            FAW = FA[:, 0:2048].rearrange("p (k o) -> p k o", k=16)
            for o in range(NOC_IN):
                ow = 128 if o < 65 else 32
                ARW = ARW_[o % 2]
                STG = STG_[o % 2]
                pb = 4 * (o % 2)
                p.dma("sync", FAW[:, :, 0:ow], W_IN_R[:, :, o * 128:o * 128 + ow])
                cp("vector", ARW[:, :, 0:ow], FAW[:, :, 0:ow])
                for t in range(4):
                    for k in range(16):
                        mm(P[pb + t][0:ow, :], ARW[:, k, 0:ow],
                           XLN[:, k * SEQ + t * 512:k * SEQ + (t + 1) * 512],
                           start=(k == 0), stop=(k == 15))
                if o < 32:
                    for t in range(4):
                        act("scalar", STG[t * 512:(t + 1) * 512] if False else STG[:, t * 512:(t + 1) * 512], P[pb + t][:, :], AF.Silu)
                    p.dma("sync", SILUZ[o * 128:(o + 1) * 128, :], STG[:, 0:2048])
                elif o < 65:
                    for t in range(4):
                        cp("vector", STG[:, t * 512:(t + 1) * 512], P[pb + t][:, :])
                    p.dma("sync", XBCFM[(o - 32) * 128:(o - 31) * 128, :], STG[:, 0:2048])
                else:
                    for t in range(4):
                        cp("vector", DTRAW[:, t * 512:(t + 1) * 512], P[pb + t][0:32, :])

        gtasks = []
        if "s8" in st_set:
            WGT_ = [XLN[:, 0:5504], XLN[:, 6144:11648]]
            TMPG = XLN[:, 12288:17792]
            IXU_ = [XLN[:, 18432:24064].bitcast(U16), XLN[:, 18432:24064].bitcast(U16)]
            STG16 = XLN[:, 24576:30080].bitcast(U16)
            I32B = XLN[:, 30720:32768].bitcast(I32)

            def _stage_task(IDXD, STAGED, r, width):
                def go():
                    for c0 in range(0, width, 1024):
                        cw = min(1024, width - c0)
                        p.dma("sync", I32B[:, 0:cw],
                              IDXD[r * 128:(r + 1) * 128, c0:c0 + cw])
                        cp("vector", STG16[:, c0:c0 + cw], I32B[:, 0:cw])
                    p.dma("sync", STAGED[r * 128:(r + 1) * 128, :], STG16[:, 0:width])
                return go

            def _gath_task(STAGED, OUTD, k, tw):
                def go():
                    iu = IXU_[k % 2]
                    wg = WGT_[k % 2]
                    p.dmat(iu[:, 0:tw], STAGED[:, k * 128:(k + 1) * 128])
                    ts("vector", wg[:, 0:tw], iu[:, 0:tw], 0.0, LUTB[:, 0:1],
                       ALU.is_equal, ALU.mult)
                    for j in range(1, 16):
                        ts("vector", TMPG[:, 0:tw], iu[:, 0:tw], float(j),
                           LUTB[:, j:j + 1], ALU.is_equal, ALU.mult)
                        tt("vector", wg[:, 0:tw], wg[:, 0:tw], TMPG[:, 0:tw], ALU.add)
                    p.dma("sync", OUTD[k * 128:(k + 1) * 128, :], wg[:, 0:tw])
                return go

            for (IDXD, STAGED, OUTD, nr, wid, nc_) in (
                    (GIDX, IXA, GW, 43, 2048, 16),
                    (UIDX, IXA, UW, 43, 2048, 16),
                    (DIDX, IXB, DW, 16, 5504, 43)):
                for r in range(nr):
                    gtasks.append(_stage_task(IDXD, STAGED, r, wid))
                for k in range(nc_):
                    gtasks.append(_gath_task(STAGED, OUTD, k, 5504 if nr == 43 else 2048))

        def pump(n):
            for _ in range(n):
                if gtasks:
                    gtasks.pop(0)()

        if "s3" in st_set:
            for c in range(33):
                xb = 2052 * (c % 2)
                sb_ = 8192 + 2048 * (c % 2)
                fb = 2052 + 4 * (c % 2)
                pb = 4 * (c % 2)
                p.dma("sync", FA[:, fb:fb + 4], CONV_W[c])
                memset("vector", AR[:, xb:xb + 3], 0.0)
                p.dma("sync", AR[:, xb + 3:xb + 2051], XBCFM[c * 128:(c + 1) * 128, :])
                dg = [WBF[:, k, :] for k in range(4)] if c % 2 == 0 else \
                     [SSD1[:, k * 128:(k + 1) * 128] for k in range(4)]
                for k in range(4):
                    ts("vector", dg[k], IDB[:, :], FA[:, fb + k:fb + k + 1],
                       None, ALU.mult)
                for t in range(4):
                    for k in range(4):
                        mm(P[pb + t][:, :], dg[k],
                           AR[:, xb + t * 512 + k:xb + t * 512 + k + 512],
                           start=(k == 0), stop=(k == 3))
                for t in range(4):
                    act("scalar", AR[:, sb_ + t * 512:sb_ + (t + 1) * 512],
                        P[pb + t][:, :], AF.Silu, bias=CBC[:, c:c + 1])
                p.dma("sync", XCFM[c * 128:(c + 1) * 128, :], AR[:, sb_:sb_ + 2048])
                pump(PUMP_S3)

        if "s4" in st_set:
            memset("vector", STATE[:, :], 0.0)
            for i in range(16):
                tr(P[5][:, 0:32], DTRAW[:, i * 128:(i + 1) * 128], IDF[0:32, 0:32])
                sl = slice(i * 32, (i + 1) * 32)
                tt("vector", FA[:, 0:32], P[5][:, 0:32], DTBC[:, :], ALU.add)
                act("scalar", FA[:, 32:64], FA[:, 0:32], AF.Abs)
                act("scalar", FA[:, 32:64], FA[:, 32:64], AF.Exp, scale=-1.0)
                act("scalar", FA[:, 32:64], FA[:, 32:64], AF.Ln, bias=1.0)
                act("scalar", FA[:, 64:96], FA[:, 0:32], AF.Relu)
                tt("vector", DTT[:, sl], FA[:, 32:64], FA[:, 64:96], ALU.add)
                act("scalar", LNDT[:, sl], DTT[:, sl], AF.Ln)
                tt("vector", ADTT[:, sl], DTT[:, sl], ABC[:, :], ALU.mult)
            for c in range(8):
                s0 = ADTT[:, (2 * c) * 32:(2 * c + 1) * 32]
                s1 = ADTT[:, (2 * c + 1) * 32:(2 * c + 2) * 32]
                mm(P[5][:, 0:32], TRI[:, :], s0, True, True)
                cp("vector", ACUM[:, (2 * c) * 32:(2 * c + 1) * 32], P[5][:, 0:32])
                mm(P[5][:, 32:64], ONEF[:, :], s0, True, False)
                mm(P[5][:, 32:64], TRI[:, :], s1, False, True)
                cp("vector", ACUM[:, (2 * c + 1) * 32:(2 * c + 2) * 32], P[5][:, 32:64])
            for i in range(16):
                tr(P[6][0:32, 0:128], ACUM[:, i * 32:(i + 1) * 32], IDF[:, :])
                cp("vector", ACFM[:, i * 128:(i + 1) * 128], P[6][0:32, 0:128])
                pump(PUMP_S4)
            p.dma("sync", CFMS[:, :], XCFM[4160:4224, :])
            p.dma("sync", BCF[0:64, :], XCFM[4096:4160, :])
            for i in range(16):
                p.dmat(BCT[:, i * 128:(i + 1) * 128],
                       XCFM[4096:4224, i * 128:(i + 1) * 128])

        if "s5" in st_set:
            iota(IOT[0:32, 0:128], [[0, 128]], 0, 1)
            XCR = XCFM[0:4096, :].rearrange("(hc p) t -> p hc t", p=128)
            SZR = SILUZ.rearrange("(hc p) t -> p hc t", p=128)
            YRR = YRMS.rearrange("(hc p) t -> p hc t", p=128)
            for c in range(8):
                csl = slice(c * 256, (c + 1) * 256)
                for st_i in range(2):
                    p.dmat(AR[:, st_i * 4096:(st_i + 1) * 4096],
                           XCFM[0:4096, c * 256 + st_i * 128:c * 256 + (st_i + 1) * 128])
                p.dma("sync", AR[:, 8192:16384].rearrange("p (hc t) -> p hc t", hc=32),
                      XCR[:, :, csl])
                p.dma("sync", AR[:, 16384:24576].rearrange("p (hc t) -> p hc t", hc=32),
                      SZR[:, :, csl])
                for st_i in range(2):
                    mm(P[1][:, st_i * 256:(st_i + 1) * 256],
                       BCF[0:64, c * 256 + st_i * 128:c * 256 + (st_i + 1) * 128],
                       CFMS[0:64, csl], True, True)
                    tt("vector", SSD1[:, st_i * 256:(st_i + 1) * 256],
                       P[1][:, st_i * 256:(st_i + 1) * 256],
                       MSK[:, st_i * 256:(st_i + 1) * 256], ALU.mult)
                for h in range(32):
                    par = h % 2
                    tx = [(2 * c) * 32 + h, (2 * c + 1) * 32 + h]
                    xtm = [AR[:, st_i * 4096 + h * 128:st_i * 4096 + (h + 1) * 128]
                           for st_i in range(2)]
                    Pb = P[0] if par == 0 else P[5]
                    Py = P[2] if par == 0 else P[6]
                    Ps = P[3] if par == 0 else P[7]
                    dif = FA[:, 0:512] if par == 0 else FA[:, 768:1280]
                    ybuf = FA[:, 512:768] if par == 0 else FA[:, 1280:1536]
                    dcol = FA[:, 2560 + 2 * par:2562 + 2 * par]
                    acol = FA[0:64, 2566 + par:2567 + par]
                    if par == 0:
                        Ebuf = SSD1[:, 512:1024]
                        Mbuf = SSD1[:, 1024:1536]
                        E64 = SSD1[0:64, 1536:1792]
                        CSc = SSD1[0:64, 1792:2048]
                        sqb = FA[:, 2304:2432].bitcast(BF16)
                        bdsl = 0
                    else:
                        Ebuf = FA[:, 1536:1792].bitcast(BF16)
                        Mbuf = FA[:, 1792:2048].bitcast(BF16)
                        E64 = FA[0:64, 2048:2176].bitcast(BF16)
                        CSc = FA[0:64, 2176:2304].bitcast(BF16)
                        sqb = FA[:, 2432:2560].bitcast(BF16)
                        bdsl = 1
                    selp = SELH[:, par * 128:(par + 1) * 128]
                    ts("vector", selp, IOT[0:32, 0:128], float(h), None, ALU.is_equal)
                    mm(Pb[:, 0:256], selp, ACFM[:, csl], True, True)
                    for st_i in range(2):
                        ts("vector", dif[:, st_i * 256:(st_i + 1) * 256], Pb[:, 0:256],
                           ACUM[:, tx[st_i]:tx[st_i] + 1], 0.0, ALU.subtract, ALU.min)
                        act("scalar", Ebuf[:, st_i * 256:(st_i + 1) * 256],
                            dif[:, st_i * 256:(st_i + 1) * 256],
                            AF.Exp, bias=LNDT[:, tx[st_i]:tx[st_i] + 1])
                        tt("vector", Mbuf[:, st_i * 256:(st_i + 1) * 256],
                           Ebuf[:, st_i * 256:(st_i + 1) * 256],
                           SSD1[:, st_i * 256:(st_i + 1) * 256], ALU.mult)
                    mm(Py[:, 0:256], xtm[0], Mbuf[:, 0:256], True, False)
                    mm(Py[:, 0:256], xtm[1], Mbuf[:, 256:512], False, False)
                    cp("vector", STATEB[:, par * 128:(par + 1) * 128],
                       STATE[:, h * 128:(h + 1) * 128])
                    act("scalar", E64[:, :], Pb[0:64, 0:256], AF.Exp)
                    tt("vector", CSc[:, :], E64[:, :], CFMS[0:64, csl], ALU.mult)
                    mm(Py[:, 0:256], STATEB[:, par * 128:(par + 1) * 128],
                       CSc[:, :], False, True)
                    for st_i in range(2):
                        ts("vector", dcol[:, st_i:st_i + 1], Pb[:, 255:256],
                           ACUM[:, tx[st_i]:tx[st_i] + 1],
                           LNDT[:, tx[st_i]:tx[st_i] + 1], ALU.subtract, ALU.add)
                        act("scalar", dcol[:, st_i:st_i + 1],
                            dcol[:, st_i:st_i + 1], AF.Exp)
                        ts("vector", WBF[:, bdsl, st_i * 64:(st_i + 1) * 64],
                           BCT[:, (2 * c + st_i) * 128:(2 * c + st_i) * 128 + 64],
                           dcol[:, st_i:st_i + 1], None, ALU.mult)
                        mm(Ps[0:64, 0:128], WBF[:, bdsl, st_i * 64:(st_i + 1) * 64],
                           xtm[st_i], start=(st_i == 0), stop=(st_i == 1))
                    stt("vector", ybuf[:, :], AR[:, 8192 + h * 256:8192 + (h + 1) * 256],
                        DBC[:, h:h + 1], Py[:, 0:256], ALU.mult, ALU.add)
                    tt("vector", AR[:, 24576 + h * 256:24576 + (h + 1) * 256],
                       ybuf[:, :], AR[:, 16384 + h * 256:16384 + (h + 1) * 256],
                       ALU.mult)
                    act("scalar", sqb[:, :],
                        AR[:, 24576 + h * 256:24576 + (h + 1) * 256], AF.Square)
                    mm(P[4][:, 0:256], ONEB[:, :], sqb[:, :],
                       start=(h == 0), stop=(h == 31))
                    act("scalar", acol[:, :], Pb[0:64, 255:256], AF.Exp)
                    stt("vector", STATE[:, h * 128:(h + 1) * 128],
                        STATE[:, h * 128:(h + 1) * 128], acol[:, :],
                        Ps[0:64, 0:128], ALU.mult, ALU.add)
                ts("vector", FA[:, 0:256], P[4][:, 0:256], float(1.0 / DIN), EPS,
                   ALU.mult, ALU.add)
                act("scalar", FA[:, 0:256], FA[:, 0:256], AF.Ln)
                act("scalar", FA[:, 0:256], FA[:, 0:256], AF.Exp, scale=-0.5)
                for h in range(32):
                    stt("vector", AR[:, h * 256:(h + 1) * 256],
                        AR[:, 24576 + h * 256:24576 + (h + 1) * 256],
                        NWC[:, h:h + 1], FA[:, 0:256], ALU.mult, ALU.mult)
                p.dma("sync", YRR[:, :, csl],
                      AR[:, 0:8192].rearrange("p (hc t) -> p hc t", hc=32))
                pump(PUMP_S5)

        if "s6" in st_set:
            W_OUT_R = W_OUT.rearrange("(kc p) o -> p kc o", p=128)
            ARW2 = AR[:, 0:8192].rearrange("p (ko o) -> p ko o", o=128)
            for op_ in range(8):
                FAW6 = FA[:, 0:2048].rearrange("p (k o) -> p k o", k=16)
                for o2 in range(2):
                    o = op_ * 2 + o2
                    for hf in range(2):
                        p.dma("sync", FAW6[:, :, :],
                              W_OUT_R[:, hf * 16:(hf + 1) * 16, o * 128:(o + 1) * 128])
                        cp("vector", ARW2[:, o2 * 32 + hf * 16:o2 * 32 + (hf + 1) * 16, :],
                           FAW6[:, :, :])
                for k in range(32):
                    p.dma("sync", AR[:, 16384:18432], YRMS[k * 128:(k + 1) * 128, :])
                    for o2 in range(2):
                        for t in range(4):
                            mm(P[o2 * 4 + t][:, :], ARW2[:, o2 * 32 + k, :],
                               AR[:, 16384 + t * 512:16384 + (t + 1) * 512],
                               start=(k == 0), stop=(k == 31))
                for o2 in range(2):
                    o = op_ * 2 + o2
                    for t in range(4):
                        cp("vector", AR[:, 18432 + t * 512:18432 + (t + 1) * 512],
                           P[o2 * 4 + t][:, :])
                    p.dma("sync", BOUT[o * 128:(o + 1) * 128, :], AR[:, 18432:20480])
                pump(PUMP_S6)

        def resid_ln(xold, xnew, write_ln):
            for t in range(16):
                p.dmat(AR[:, 0:2048], BOUT[:, t * 128:(t + 1) * 128])
                p.dma("sync", FA[:, 0:2048], xold[t * 128:(t + 1) * 128, :])
                tt("vector", FA[:, 0:2048], FA[:, 0:2048], AR[:, 0:2048], ALU.add)
                p.dma("sync", xnew[t * 128:(t + 1) * 128, :], FA[:, 0:2048])
                if write_ln:
                    ln_tile(FA[:, 0:2048], AR[:, 0:2048])
                    p.dma("sync", XLNTM[t * 128:(t + 1) * 128, :], AR[:, 0:2048])

        if "s7" in st_set:
            resid_ln(U, X1, True)

        if "s8" in st_set:
            pump(10 ** 9)
            load_xln_fm()
            GW_R = GW.rearrange("(kc p) o -> p kc o", p=128)
            UW_R = UW.rearrange("(kc p) o -> p kc o", p=128)
            ARG = AR[:, 22016:24064].rearrange("p (k o) -> p k o", k=16)
            ARU = AR[:, 24064:26112].rearrange("p (k o) -> p k o", k=16)
            HQ = AR[:, 0:22016]
            for q in range(4):
                qsl = lambda k: slice(k * SEQ + q * 512, k * SEQ + (q + 1) * 512)
                for o in range(43):
                    p.dma("sync", ARG[:, :, :], GW_R[:, :, o * 128:(o + 1) * 128])
                    p.dma("sync", ARU[:, :, :], UW_R[:, :, o * 128:(o + 1) * 128])
                    for k in range(16):
                        mm(P[0][:, :], ARG[:, k, :], XLN[:, qsl(k)],
                           start=(k == 0), stop=(k == 15))
                        mm(P[1][:, :], ARU[:, k, :], XLN[:, qsl(k)],
                           start=(k == 0), stop=(k == 15))
                    act("scalar", AR[:, 26112:26624], P[0][:, :], AF.Silu)
                    tt("vector", HQ[:, o * 512:(o + 1) * 512], AR[:, 26112:26624],
                       P[1][:, :], ALU.mult)
                for dh in range(2):
                    for o in range(43):
                        p.dma("sync", AR[:, 26112:27136],
                              DW[o * 128:(o + 1) * 128,
                                 dh * 1024:(dh + 1) * 1024])
                        dwv = AR[:, 26112:27136].rearrange("p (dc d) -> p dc d", d=128)
                        for dc in range(8):
                            mm(P[dc][:, :], dwv[:, dc, :], HQ[:, o * 512:(o + 1) * 512],
                               start=(o == 0), stop=(o == 42))
                    for dc in range(8):
                        d = dh * 8 + dc
                        cp("vector", AR[:, 27136:27648], P[dc][:, :])
                        p.dma("sync", BOUT[d * 128:(d + 1) * 128, q * 512:(q + 1) * 512],
                              AR[:, 27136:27648])

        if "s9" in st_set:
            resid_ln(X1, X2, True)


        if "s10" in st_set:
            load_xln_fm()
            COSR_ = AR[:, 12288:14336]
            SINS_ = AR[:, 14336:16384]
            iota(IOT[:, 0:1], [[0, 1]], 0, 1)
            cp("vector", SHC[:, :], IOT[:, 0:1])
            ts("vector", FA[:, 0:1], SHC[:, :], 64.0, -64.0,
               ALU.is_ge, ALU.mult)
            tt("vector", SHC[:, :], SHC[:, :], FA[:, 0:1], ALU.add)
            ts("vector", SHC[:, :], SHC[:, :],
               -float(np.log(1e4) / 64.0), None, ALU.mult)
            act("scalar", SHC[:, :], SHC[:, :], AF.Exp)
            for th in range(2):
                iota(IOT[:, 0:1024], [[1, 1024]], th * 1024, 0)
                cp("vector", FA[:, 0:1024], IOT[:, 0:1024])
                ts("vector", FA[:, 0:1024], FA[:, 0:1024], SHC[:, :],
                   None, ALU.mult)
                sl = slice(th * 1024, (th + 1) * 1024)
                for (dst, shift) in ((COSR_, 0.25), (SINS_, 0.0)):
                    ts("vector", FA[:, 1024:2048], FA[:, 0:1024],
                       float(1.0 / (2 * PI)), float(shift), ALU.mult, ALU.add)
                    cp("vector", IOT[:, 0:1024], FA[:, 1024:2048])
                    cp("vector", FA[:, 2048:3072], IOT[:, 0:1024])
                    tt("vector", FA[:, 1024:2048], FA[:, 1024:2048],
                       FA[:, 2048:3072], ALU.subtract)
                    ts("vector", FA[:, 1024:2048], FA[:, 1024:2048],
                       float(2 * PI), None, ALU.mult)
                    act("scalar", dst[:, sl], FA[:, 1024:2048], AF.Sin)
            ts("vector", SINS_[0:64, :], SINS_[0:64, :], -1.0, None, ALU.mult)

            RTBD = WBF[:, 2, :]
            cp("vector", RTBD[0:128, 0:64], RTB[:, :])
            cp("vector", RTBD[0:128, 64:128], RTB[:, :])
            memset("vector", SHC[:, :], 0.25)
            memset("vector", SHC[0:64, :], 0.0)

            def proj(Wdram, n_out, bias_cols, dst, vscale=None):
                for o in range(n_out):
                    p.dma("sync", FA[:, 0:2048], Wdram[o * 128:(o + 1) * 128, :])
                    for k in range(16):
                        tr(P[5][:, 0:128], FA[:, k * 128:(k + 1) * 128], IDF[:, :])
                        cp("vector", AR[:, k * 128:(k + 1) * 128], P[5][:, 0:128])
                    for k in range(16):
                        for t in range(4):
                            mm(P[t][:, :], AR[:, k * 128:(k + 1) * 128],
                               XLN[:, k * SEQ + t * 512:k * SEQ + (t + 1) * 512],
                               start=(k == 0), stop=(k == 15))
                    for t in range(4):
                        if vscale is None:
                            ts("vector", AR[:, 2048 + t * 512:2048 + (t + 1) * 512],
                               P[t][:, :], bias_cols[:, o:o + 1], None, ALU.add)
                        else:
                            ts("vector", AR[:, 2048 + t * 512:2048 + (t + 1) * 512],
                               P[t][:, :], bias_cols[:, o:o + 1], vscale,
                               ALU.add, ALU.mult)
                    p.dma("sync", dst[o * 128:(o + 1) * 128, :], AR[:, 2048:4096])

            proj(WQ, 16, BQC, QFM)
            proj(WK, 8, BKC, KFM)
            proj(WV, 8, BVC, VSC, vscale=float(2.0 / 128.0))

            def feats_head(srcd, h, dstd):
                p.dma("sync", AR[:, 4096:6144], srcd[h * 128:(h + 1) * 128, :])
                p.dma("sync", AR[0:64, 6144:8192], AR[64:128, 4096:6144])
                p.dma("sync", AR[64:128, 6144:8192], AR[0:64, 4096:6144])
                tt("vector", AR[:, 8192:10240], AR[:, 4096:6144], COSR_, ALU.mult)
                tt("vector", AR[:, 10240:12288], AR[:, 6144:8192], SINS_, ALU.mult)
                tt("vector", AR[:, 4096:6144], AR[:, 8192:10240],
                   AR[:, 10240:12288], ALU.add)
                act("scalar", AR[:, 8192:10240], AR[:, 4096:6144], AF.Square)
                for t in range(4):
                    mm(P[t][:, :], ONEB[:, :], AR[:, 8192 + t * 512:8192 + (t + 1) * 512],
                       True, True)
                for t in range(4):
                    ts("vector", FA[:, 0:512], P[t][:, :], 1e-12, None, ALU.max)
                    act("scalar", FA[:, 0:512], FA[:, 0:512], AF.Ln)
                    act("scalar", FA[:, 0:512], FA[:, 0:512], AF.Exp, scale=-0.5)
                    mm(P[4][:, 0:512], RTBD[:, :],
                       AR[:, 4096 + t * 512:4096 + (t + 1) * 512], True, True)
                    tt("vector", FA[:, 512:1024], P[4][:, 0:512], FA[:, 0:512], ALU.mult)
                    ts("vector", FA[:, 1024:1536], FA[:, 512:1024],
                       float(1.0 / (2 * PI)), SHC[:, :], ALU.mult, ALU.add)
                    cp("vector", IOT[:, 0:512], FA[:, 1024:1536])
                    cp("vector", FA[:, 1536:2048], IOT[:, 0:512])
                    tt("vector", FA[:, 1024:1536], FA[:, 1024:1536],
                       FA[:, 1536:2048], ALU.subtract)
                    ts("vector", FA[:, 1024:1536], FA[:, 1024:1536],
                       float(2 * PI), None, ALU.mult)
                    act("scalar", AR[:, 10240 + t * 512:10240 + (t + 1) * 512],
                        FA[:, 1024:1536], AF.Sin)
                p.dma("sync", dstd[h * 128:(h + 1) * 128, :], AR[:, 10240:12288])

            for h in range(16):
                feats_head(QFM, h, QPFM)
            for h in range(8):
                feats_head(KFM, h, KPFM)

            for i in range(16):
                p.dmat(AR[:, 0:2048], QPFM[:, i * 128:(i + 1) * 128])
                p.dmat(AR[:, 2048:3072], KPFM[:, i * 128:(i + 1) * 128])
                p.dmat(AR[:, 3072:4096], VSC[:, i * 128:(i + 1) * 128])
                for h in range(16):
                    for j in range(8):
                        p.op("vector", lambda e, h=h, j=j: e.scalar_tensor_tensor(
                            AR[:, 4096:4224], AR[:, h * 128:(h + 1) * 128], 1.0,
                            AR[:, 2048 + j * 128:2048 + (j + 1) * 128],
                            op0=ALU.mult, op1=ALU.mult,
                            accum_out=FA[:, 2048 + h * 8 + j:2049 + h * 8 + j]),
                            ins=[AR[:, h * 128:(h + 1) * 128],
                                 AR[:, 2048 + j * 128:2048 + (j + 1) * 128]],
                            outs=[AR[:, 4096:4224],
                                  FA[:, 2048 + h * 8 + j:2049 + h * 8 + j]])
                for h in range(16):
                    acc = FA[:, h * 128:(h + 1) * 128]
                    ts("vector", acc, AR[:, 3072:3200],
                       FA[:, 2048 + h * 8:2049 + h * 8], None, ALU.mult)
                    for j in range(1, 8):
                        stt("vector", acc, AR[:, 3072 + j * 128:3072 + (j + 1) * 128],
                            FA[:, 2048 + h * 8 + j:2049 + h * 8 + j], acc,
                            ALU.mult, ALU.add)
                cp("vector", AR[:, 4096:6144], FA[:, 0:2048])
                OSCR = OTM.rearrange("(h ii a) (b d) -> ii (a b) h d",
                                     h=16, ii=16, a=8, d=128)
                p.dma("sync", OSCR[i],
                      AR[:, 4096:6144].rearrange("p (h d) -> p h d", h=16))

            for k in range(16):
                p.dmat(XLN[:, k * SEQ:(k + 1) * SEQ], OTM[:, k * 128:(k + 1) * 128])
            proj(WO, 16, BOC, BOUT)

        if "s11" in st_set:
            resid_ln(X2, X3, True)

        if "s12" in st_set:
            load_xln_fm()
            ASB = AR[:, 0:256].rearrange("p (k r) -> p k r", k=16)
            p.dma("sync", FA[:, 2048:2304].rearrange("p (k r) -> p k r", k=16),
                  A_S.rearrange("(kc p) r -> p kc r", p=128))
            cp("vector", ASB[:, :, :], FA[:, 2048:2304].rearrange("p (k r) -> p k r", k=16))
            WRB = AR[:, 256:352].rearrange("p (k r) -> p k r", k=16)
            p.dma("sync", FA[:, 2304:2400].rearrange("p (k r) -> p k r", k=16),
                  W_ROUTER.rearrange("(kc p) r -> p kc r", p=128))
            cp("vector", WRB[:, :, :], FA[:, 2304:2400].rearrange("p (k r) -> p k r", k=16))
            ARB = AR[:, 512:2048].rearrange("p (e k r) -> p e k r", e=6, k=16)
            for e6 in range(6):
                p.dma("sync", FA[:, 2400:2656].rearrange("p (k r) -> p k r", k=16),
                      A_R.rearrange("e (kc p) r -> p e kc r", p=128)[:, e6])
                cp("vector", ARB[:, e6, :, :],
                   FA[:, 2400:2656].rearrange("p (k r) -> p k r", k=16))
            BSB = AR[0:16, 2048:4096]
            p.dma("sync", FA[0:16, 0:2048], B_S)
            cp("vector", BSB, FA[0:16, 0:2048])
            BRB = AR[0:16, 4096:16384].rearrange("p (e d) -> p e d", e=6)
            for e6 in range(6):
                p.dma("sync", FA[0:16, 0:2048], B_R[e6])
                cp("vector", BRB[:, e6, :], FA[0:16, 0:2048])
            for t in range(4):
                for k in range(16):
                    mm(P[4 + t][0:6, :], WRB[:, k, :],
                       XLN[:, k * SEQ + t * 512:k * SEQ + (t + 1) * 512],
                       start=(k == 0), stop=(k == 15))
            for t in range(4):
                ts("vector", FA[0:6, t * 512:(t + 1) * 512], P[4 + t][0:6, :],
                   BRC[:, :], None, ALU.add)
            for i in range(16):
                isl = slice(i * 128, (i + 1) * 128)
                tr(P[0][:, 0:6], FA[0:6, isl], IDF[0:6, 0:6])
                cp("vector", FA[:, 2048:2054], P[0][:, 0:6])
                p.op("vector", lambda e: e.reduce_max(FA[:, 2054:2055], FA[:, 2048:2054], axis=AXX),
                     ins=[FA[:, 2048:2054]], outs=[FA[:, 2054:2055]])
                ts("vector", FA[:, 2048:2054], FA[:, 2048:2054], FA[:, 2054:2055],
                   None, ALU.subtract)
                act("scalar", FA[:, 2048:2054], FA[:, 2048:2054], AF.Exp)
                p.op("vector", lambda e: e.reduce_sum(FA[:, 2055:2056], FA[:, 2048:2054], axis=AXX),
                     ins=[FA[:, 2048:2054]], outs=[FA[:, 2055:2056]])
                p.op("vector", lambda e: e.reciprocal(FA[:, 2056:2057], FA[:, 2055:2056]),
                     ins=[FA[:, 2055:2056]], outs=[FA[:, 2056:2057]])
                ts("vector", FA[:, 2048:2054], FA[:, 2048:2054], FA[:, 2056:2057],
                   None, ALU.mult)
                p.op("vector", lambda e: e.reduce_max(FA[:, 2054:2055], FA[:, 2048:2054], axis=AXX),
                     ins=[FA[:, 2048:2054]], outs=[FA[:, 2054:2055]])
                ts("vector", FA[:, 2057:2063], FA[:, 2048:2054], FA[:, 2054:2055],
                   -1.0, ALU.is_equal, ALU.mult)
                ts("vector", FA[:, 2057:2063], FA[:, 2057:2063], 1.0, None, ALU.add)
                tt("vector", FA[:, 2057:2063], FA[:, 2057:2063], FA[:, 2048:2054], ALU.mult)
                p.op("vector", lambda e: e.reduce_max(FA[:, 2054:2055], FA[:, 2057:2063], axis=AXX),
                     ins=[FA[:, 2057:2063]], outs=[FA[:, 2054:2055]])
                ts("vector", FA[:, 2057:2063], FA[:, 2048:2054], FA[:, 2054:2055],
                   None, ALU.is_ge)
                tt("vector", FA[:, 2048:2054], FA[:, 2048:2054], FA[:, 2057:2063], ALU.mult)
                tr(P[0][0:6, 0:128], FA[:, 2048:2054], IDF[:, :])
                cp("vector", FA[0:6, isl], P[0][0:6, 0:128])
            for t in range(4):
                for k in range(16):
                    mm(P[4 + t][0:16, :], ASB[:, k, :],
                       XLN[:, k * SEQ + t * 512:k * SEQ + (t + 1) * 512],
                       start=(k == 0), stop=(k == 15))
            H16 = AR[0:16, 16384:18432]
            for t in range(4):
                cp("vector", H16[:, t * 512:(t + 1) * 512], P[4 + t][0:16, :])
            ART = AR[0:16, 18432:18944]
            for e6 in range(6):
                for t in range(4):
                    for k in range(16):
                        mm(P[0][0:16, :], ARB[:, e6, k, :],
                           XLN[:, k * SEQ + t * 512:k * SEQ + (t + 1) * 512],
                           start=(k == 0), stop=(k == 15))
                    mm(P[1][0:16, :], SEL6[:, e6 * 16:(e6 + 1) * 16],
                       FA[0:6, t * 512:(t + 1) * 512], True, True)
                    cp("vector", ART[:, 0:512], P[0][0:16, :])
                    tt("vector", AR[0:16, 18944 + e6 * 2048 + t * 512:
                                 18944 + e6 * 2048 + (t + 1) * 512],
                       ART[:, 0:512], P[1][0:16, :], ALU.mult)
            W_BASE_R = W_BASE.rearrange("(kc p) o -> p kc o", p=128)
            WBV = SSD1[:, 0:2048].rearrange("p (k o) -> p k o", k=16)
            STG12 = FA[:, 1024:2048].bitcast(BF16)
            FAW12 = FA[:, 0:1024].rearrange("p (k o) -> p k o", k=8)
            for o in range(16):
                for hf in range(2):
                    p.dma("sync", FAW12[:, :, :],
                          W_BASE_R[:, hf * 8:(hf + 1) * 8, o * 128:(o + 1) * 128])
                    cp("vector", WBV[:, hf * 8:(hf + 1) * 8, :], FAW12[:, :, :])
                for t in range(4):
                    tsl = slice(t * 512, (t + 1) * 512)
                    for k in range(16):
                        mm(P[t][:, :], WBV[:, k, :],
                           XLN[:, k * SEQ + t * 512:k * SEQ + (t + 1) * 512],
                           start=(k == 0), stop=False)
                    mm(P[t][:, :], BSB[:, o * 128:(o + 1) * 128], H16[:, tsl],
                       start=False, stop=False)
                    for e6 in range(6):
                        mm(P[t][:, :], BRB[:, e6, o * 128:(o + 1) * 128],
                           AR[0:16, 18944 + e6 * 2048 + t * 512:
                              18944 + e6 * 2048 + (t + 1) * 512],
                           start=False, stop=(e6 == 5))
                    ts("vector", STG12[:, tsl], P[t][:, :], BBC[:, o:o + 1],
                       None, ALU.add)
                p.dma("sync", BOUT[o * 128:(o + 1) * 128, :], STG12[:, 0:2048])

        if "s13" in st_set:
            resid_ln(X3, Y, False)

        p.emit(block, nc, sems)

    return nc


def _core_inputs(inputs, core):
    g = lambda k: np.ascontiguousarray(inputs[k])
    m = {
        "u": np.ascontiguousarray(inputs["L0_attn_u"][core]),
        "w_in": g("L0_attn_W_in"),
        "conv_w": g("L0_attn_conv_w").reshape(33, 128, 4),
        "conv_b": g("L0_attn_conv_b").reshape(33, 128, 1),
        "dt_bias": g("L0_attn_dt_bias").reshape(1, NH),
        "a_log": g("L0_attn_A_log").reshape(1, NH),
        "dd": g("L0_attn_D").reshape(1, NH),
        "norm_w": g("L0_attn_norm_w").reshape(32, 128, 1),
        "w_out": g("L0_attn_W_out"),
        "lutv": g("L0_ffn_lut").reshape(1, 16),
        "gidx": g("L0_ffn_gate_idx"),
        "uidx": g("L0_ffn_up_idx"),
        "didx": g("L0_ffn_down_idx"),
        "wq": g("L1_attn_Wq"),
        "wk": g("L1_attn_Wk"),
        "wv": g("L1_attn_Wv"),
        "wo": g("L1_attn_Wo"),
        "bq": g("L1_attn_bq").reshape(16, 128, 1),
        "bk": g("L1_attn_bk").reshape(8, 128, 1),
        "bv": g("L1_attn_bv").reshape(8, 128, 1),
        "bo": g("L1_attn_bo").reshape(16, 128, 1),
        "rmat": g("L1_attn_rand_matrix"),
        "w_base": g("L1_ffn_W_base"),
        "b_base": g("L1_ffn_b_base").reshape(16, 128, 1),
        "a_s": g("L1_ffn_A_s"),
        "b_s": g("L1_ffn_B_s"),
        "a_r": g("L1_ffn_A_r"),
        "b_r": g("L1_ffn_B_r"),
        "w_router": g("L1_ffn_W_router"),
        "b_router": g("L1_ffn_b_router").reshape(6, 1),
    }
    return m


def kernel(**inputs):
    nc = build_program()
    in_maps = [_core_inputs(inputs, c) for c in range(8)]
    res = run_bass_kernel_spmd(nc, in_maps, core_ids=list(range(8)))
    return np.stack([r["y"] for r in res.results], axis=0)


# version 13 (final)
# speedup vs baseline: 1.3555x; 1.3555x over previous
import numpy as np
from contextlib import ExitStack
import concourse.bass as bass
import concourse.mybir as mybir
from concourse.bass_utils import run_bass_kernel_spmd

F32 = mybir.dt.float32
BF16 = mybir.dt.bfloat16
I32 = mybir.dt.int32
U16 = mybir.dt.uint16
AF = mybir.ActivationFunctionType
ALU = mybir.AluOpType
AXX = mybir.AxisListType.X

SEQ = 2048
DM = 2048
DIN = 4096
CONVD = 4224
DPROJ = 8352
NH = 32
HD = 128
DST = 64
CHUNK = 256
NCHUNK = SEQ // CHUNK
INTER = 5504
NOC_IN = DPROJ // 128 + 1
EPS = 1e-5
PI = float(np.pi)

ALL_STAGES = ("s0", "s1", "s2", "s3", "s4", "s5", "s6", "s7",
              "s8", "s9", "s10", "s11", "s12", "s13")
PUMP_S3 = 8
PUMP_S4 = 4
PUMP_S5 = 8
PUMP_S6 = 24
PUMP_H = 2


_DSZ = {mybir.dt.float32: 4, mybir.dt.bfloat16: 2, mybir.dt.int32: 4,
        mybir.dt.uint16: 2}

ENGS = ("sync", "scalar", "vector", "tensor", "gpsimd")


class Sched:

    PB = 32
    FB = 512

    def __init__(self):
        self.steps = []
        self.cells = {}
        self.cum = {e: 0 for e in ENGS}
        self.tinfo = {}
        self.last_cast = {}

    def reg(self, name, space, fs_bytes):
        self.tinfo[name] = (space, fs_bytes)

    def _cells(self, ap):
        t = ap.tensor
        name = t.name
        esz = _DSZ[ap.dtype]
        off_b = ap.offset * esz
        space, fs = self.tinfo.get(name, ("dram", None))
        pspan_b = 0
        fspan_b = 0
        if space == "dram":
            lo = off_b
            hi = off_b
            for st, ct in ap.ap:
                hi += abs(st) * esz * (ct - 1)
            hi += esz
            return [(name, 0, fb) for fb in range(lo // self.FB, (hi - 1) // self.FB + 1)]
        p0 = off_b // fs
        f0 = off_b % fs
        for st, ct in ap.ap:
            sb = st * esz
            if sb >= fs or sb <= -fs:
                pspan_b += abs(sb) // fs * (ct - 1)
            else:
                fspan_b += abs(sb) * (ct - 1)
        p1 = p0 + pspan_b
        f1 = f0 + fspan_b + esz
        out = []
        for pg in range(p0 // self.PB, p1 // self.PB + 1):
            for fb in range(f0 // self.FB, (f1 - 1) // self.FB + 1):
                out.append((name, pg, fb))
        return out

    def op(self, eng, fn, ins=(), outs=(), inc=1, xguard=False, extra=()):
        rc = []
        wc = []
        for ap in ins:
            rc.extend(self._cells(ap))
        for ap in outs:
            wc.extend(self._cells(ap))
        if xguard == "r":
            rc.append(("XGUARD", 0, 0))
        elif xguard:
            wc.append(("XGUARD", 0, 0))
        self._extra = tuple(extra)
        waits = {}

        def need(e2, v):
            if e2 == eng and eng == "tensor":
                return
            if v > waits.get(e2, 0):
                waits[e2] = v

        for (e2x, vx) in self._extra:
            need(e2x, vx)
        seen = set()
        for c in rc:
            if c in seen:
                continue
            seen.add(c)
            ent = self.cells.get(c)
            if ent and ent[0]:
                need(ent[0][0], ent[0][1])
        seenw = set()
        for c in wc:
            if c in seenw:
                continue
            seenw.add(c)
            ent = self.cells.get(c)
            if ent:
                if ent[0]:
                    need(ent[0][0], ent[0][1])
                for e2, v in ent[1].items():
                    need(e2, v)
        self.cum[eng] += inc
        mycum = self.cum[eng]
        for c in seenw:
            self.cells[c] = [(eng, mycum), {}]
        for c in seen:
            if c in seenw:
                continue
            ent = self.cells.setdefault(c, [None, {}])
            if ent[1].get(eng, 0) < mycum:
                ent[1][eng] = mycum
        self.steps.append((eng, fn, inc, waits))

    def dma(self, eng, out, in_, **kw):
        g = (self.tinfo.get(out.tensor.name, ("dram",))[0] == "sbuf" and
             self.tinfo.get(in_.tensor.name, ("dram",))[0] == "sbuf")
        extra = []
        if in_.dtype != out.dtype:
            key = (in_.dtype, out.dtype)
            for k2, v2 in self.last_cast.items():
                if k2 != key:
                    extra.append(v2)
            self.op(eng, lambda e: e.dma_start(out=out, in_=in_, **kw),
                    ins=[in_], outs=[out], inc=16, xguard=g, extra=extra)
            self.last_cast[key] = (eng, self.cum[eng])
            return
        self.op(eng, lambda e: e.dma_start(out=out, in_=in_, **kw),
                ins=[in_], outs=[out], inc=16, xguard=g)

    def dmat(self, out, in_):
        self.op("sync", lambda e: e.dma_start_transpose(out, in_),
                ins=[in_], outs=[out], inc=16, xguard=True)

    def emit(self, block, nc, sems):
        done = {e: {e2: 0 for e2 in ENGS} for e in ENGS}

        def mk(eng, steps_):
            def emitf(engine):
                for (fn, inc, waits) in steps_:
                    for e2, v in waits.items():
                        if done[eng][e2] < v:
                            engine.wait_ge(sems[e2], v)
                            done[eng][e2] = v
                    fn(engine).then_inc(sems[eng], inc)
                for e2 in ENGS:
                    if e2 != eng and self.cum[e2] > done[eng][e2]:
                        engine.wait_ge(sems[e2], self.cum[e2])
            return emitf

        for name in ENGS:
            steps_ = [(fn, inc, w) for (e, fn, inc, w) in self.steps if e == name]
            if steps_ or name == "sync":
                getattr(block, name)(mk(name, steps_))


def build_program(stages=ALL_STAGES, taps=()):
    nc = bass.Bass("TRN2", target_bir_lowering=False, debug=False)
    st_set = set(stages)
    tap_set = set(taps)

    def dram(name, shape, dtype, kind="Internal"):
        if kind == "Internal" and name in tap_set:
            kind = "ExternalOutput"
        return nc.dram_tensor(name, shape, dtype, kind=kind).ap()

    U = dram("u", [SEQ, DM], F32, "ExternalInput")
    W_IN = dram("w_in", [DM, DPROJ], F32, "ExternalInput")
    CONV_W = dram("conv_w", [33, 128, 4], F32, "ExternalInput")
    CONV_B = dram("conv_b", [33, 128, 1], F32, "ExternalInput")
    DT_BIAS = dram("dt_bias", [1, NH], F32, "ExternalInput")
    A_LOG = dram("a_log", [1, NH], F32, "ExternalInput")
    DD = dram("dd", [1, NH], F32, "ExternalInput")
    NORM_W = dram("norm_w", [32, 128, 1], F32, "ExternalInput")
    W_OUT = dram("w_out", [DIN, DM], F32, "ExternalInput")
    LUTV = dram("lutv", [1, 16], F32, "ExternalInput")
    GIDX = dram("gidx", [INTER, DM], I32, "ExternalInput")
    UIDX = dram("uidx", [INTER, DM], I32, "ExternalInput")
    DIDX = dram("didx", [DM, INTER], I32, "ExternalInput")
    WQ = dram("wq", [DM, DM], F32, "ExternalInput")
    WK = dram("wk", [1024, DM], F32, "ExternalInput")
    WV = dram("wv", [1024, DM], F32, "ExternalInput")
    WO = dram("wo", [DM, DM], F32, "ExternalInput")
    BQ = dram("bq", [16, 128, 1], F32, "ExternalInput")
    BK = dram("bk", [8, 128, 1], F32, "ExternalInput")
    BV = dram("bv", [8, 128, 1], F32, "ExternalInput")
    BO = dram("bo", [16, 128, 1], F32, "ExternalInput")
    RMAT = dram("rmat", [64, 128], F32, "ExternalInput")
    W_BASE = dram("w_base", [DM, DM], F32, "ExternalInput")
    B_BASE = dram("b_base", [16, 128, 1], F32, "ExternalInput")
    A_S = dram("a_s", [DM, 16], F32, "ExternalInput")
    B_S = dram("b_s", [16, DM], F32, "ExternalInput")
    A_R = dram("a_r", [6, DM, 16], F32, "ExternalInput")
    B_R = dram("b_r", [6, 16, DM], F32, "ExternalInput")
    W_ROUTER = dram("w_router", [DM, 6], F32, "ExternalInput")
    B_ROUTER = dram("b_router", [6, 1], F32, "ExternalInput")

    Y = dram("y", [SEQ, DM], F32, "ExternalOutput")

    XLNTM = dram("xlntm", [SEQ, DM], BF16)
    SILUZ = dram("siluz", [DIN, SEQ], BF16)
    XBCFM = dram("xbcfm", [CONVD, SEQ], BF16)
    XCFM = dram("xcfm", [CONVD, SEQ], BF16)
    YRMS = dram("yrms", [DIN, SEQ], BF16)
    BOUT = dram("bout", [DM, SEQ], BF16)
    X1 = dram("x1", [SEQ, DM], F32)
    X2 = dram("x2", [SEQ, DM], F32)
    X3 = dram("x3", [SEQ, DM], F32)
    IXA = dram("ixa", [INTER, DM], U16)
    IXB = dram("ixb", [DM, INTER], U16)
    GW = dram("gw", [DM, INTER], BF16)
    UW = dram("uw", [DM, INTER], BF16)
    DW = dram("dw", [INTER, DM], BF16)
    HFF = dram("hff", [INTER, SEQ], BF16)
    WQS = dram("wqs", [DM, DM], BF16)
    WKS = dram("wks", [1024, DM], BF16)
    WVS = dram("wvs", [1024, DM], BF16)
    WOS = dram("wos", [DM, DM], BF16)
    QFM = dram("qfm", [DM, SEQ], BF16)
    KFM = dram("kfm", [1024, SEQ], BF16)
    VFM = dram("vfm", [1024, SEQ], BF16)
    QPFM = dram("qpfm", [DM, SEQ], BF16)
    KPFM = dram("kpfm", [1024, SEQ], BF16)
    VSC = dram("vsc", [1024, SEQ], BF16)
    OTM = dram("otm", [SEQ, DM], BF16)

    p = Sched()

    with ExitStack() as stk:
        E = stk.enter_context

        def SB(name, shape, dtype):
            t = E(nc.sbuf_tensor(name, shape, dtype))
            p.reg(name, "sbuf", int(np.prod(shape[1:])) * _DSZ[dtype])
            return t

        def PS(name, shape, dtype):
            t = E(nc.psum_tensor(name, shape, dtype))
            p.reg(name, "sbuf", int(np.prod(shape[1:])) * _DSZ[dtype])
            return t

        XLN = SB("xlns", [128, 16 * SEQ], BF16)
        AR = SB("arena", [128, 32768], BF16)
        FA = SB("f32a", [128, 3072], F32)
        WBF = SB("wbf", [128, 4, 128], BF16)
        IOT = SB("iot", [128, 1024], I32)
        IDF = SB("idf", [128, 128], F32)
        IDB = SB("idb", [128, 128], BF16)
        ONEB = SB("oneb", [128, 128], BF16)
        TRI = SB("tri", [128, 128], F32)
        ONEF = SB("onef", [128, 128], F32)
        ONE1 = SB("one1", [1, 128], F32)
        SELH = SB("selh", [32, 256], F32)
        SEL6 = SB("sel6", [6, 96], F32)
        MSK = SB("msk", [128, 512], BF16)
        CBC = SB("cbc", [128, 33], F32)
        NWC = SB("nwc", [128, 32], F32)
        BQC = SB("bqc", [128, 16], F32)
        BKC = SB("bkc", [128, 8], F32)
        BVC = SB("bvc", [128, 8], F32)
        BOC = SB("boc", [128, 16], F32)
        BBC = SB("bbc", [128, 16], F32)
        BRC = SB("brc", [6, 1], F32)
        LUTB = SB("lutb", [128, 16], F32)
        ROWS = SB("rows", [1, 32], F32)
        ROWS2 = SB("rows2", [1, 32], F32)
        DTBC = SB("dtbc", [128, 32], F32)
        ABC = SB("abc", [128, 32], F32)
        DBC = SB("dbc", [128, 32], F32)
        DTRAW = SB("dtraw", [32, SEQ], F32)
        DTT = SB("dtt", [128, 512], F32)
        ADTT = SB("adtt", [128, 512], F32)
        LNDT = SB("lndt", [128, 512], F32)
        ACUM = SB("acum", [128, 512], F32)
        ACFM = SB("acfm", [32, 2048], F32)
        BCF = SB("bcf", [128, 2048], BF16)
        BCT = SB("bct", [128, 2048], BF16)
        RTB = SB("rtb", [128, 64], BF16)
        SHC = SB("shc", [128, 1], F32)
        STATE = SB("state", [64, 128 * 32], F32)
        STATEB = SB("stateb", [64, 256], BF16)
        CFMS = SB("cfms", [64, 2048], BF16)
        SSD1 = SB("ssd1", [128, 2048], BF16)
        P = [PS(f"ps{i}", [128, 512], F32) for i in range(8)]
        sems = {e: E(nc.semaphore("sem_" + e)) for e in ENGS}
        block = E(nc.Block())

        APT = bass.AP

        def _aps(*xs):
            return [x for x in xs if isinstance(x, APT)]

        def ts(eng, out, in0, s1, s2, op0, op1=None):
            if op1 is None:
                p.op(eng, lambda e: e.tensor_scalar(out, in0, s1, None, op0=op0),
                     ins=_aps(in0, s1), outs=[out])
            else:
                p.op(eng, lambda e: e.tensor_scalar(out, in0, s1, s2, op0=op0, op1=op1),
                     ins=_aps(in0, s1, s2), outs=[out])

        def tt(eng, out, a, b, op):
            p.op(eng, lambda e: e.tensor_tensor(out, a, b, op=op),
                 ins=[a, b], outs=[out])

        def stt(eng, out, in0, scal, in1, op0, op1):
            p.op(eng, lambda e: e.scalar_tensor_tensor(out, in0, scal, in1,
                                                       op0=op0, op1=op1),
                 ins=_aps(in0, scal, in1), outs=[out])

        def act(eng, out, in_, func, **kw):
            aux = _aps(kw.get("bias"), kw.get("scale"), kw.get("accum_out"))
            aout = _aps(kw.get("accum_out"))
            p.op(eng, lambda e: e.activation(out, in_, func, **kw),
                 ins=_aps(in_, kw.get("bias"), kw.get("scale")), outs=[out] + aout)

        def cp(eng, out, in_):
            p.op(eng, lambda e: e.tensor_copy(out, in_), ins=[in_], outs=[out])

        def mm(out, lhsT, rhs, start, stop):
            p.op("tensor", lambda e: e.matmul(out, lhsT, rhs, start=start, stop=stop),
                 ins=[lhsT, rhs, out], outs=[out])

        def tr(out, in_, ident):
            p.op("tensor", lambda e: e.transpose(out, in_, ident),
                 ins=[in_, ident], outs=[out])

        def iota(out, pattern, base, cm):
            p.op("gpsimd", lambda e: e.iota(out, pattern=pattern, base=base,
                                            channel_multiplier=cm), outs=[out])

        def memset(eng, ap_, val):
            p.op(eng, lambda e: e.memset(ap_, val), outs=[ap_])

        if "s0" in st_set:
            iota(IOT[:, 0:128], [[1, 128]], 0, -1)
            ts("vector", IDF[:, :], IOT[:, 0:128], 0.0, None, ALU.is_equal)
            cp("vector", IDB[:, :], IDF[:, :])
            ts("vector", TRI[:, :], IOT[:, 0:128], 0.0, None, ALU.is_ge)
            memset("vector", ONEB[:, :], 1.0)
            memset("vector", ONEF[:, :], 1.0)
            memset("vector", ONE1[:, :], 1.0)
            iota(IOT[0:32, 0:128], [[0, 128]], 0, 1)
            for e6 in range(6):
                ts("vector", SEL6[:, e6 * 16:(e6 + 1) * 16], IOT[0:6, 0:16],
                   float(e6), None, ALU.is_equal)
            iota(IOT[:, 0:256], [[1, 256]], 0, -1)
            ts("vector", MSK[:, 0:256], IOT[:, 0:256], 0.0, None, ALU.is_ge)
            iota(IOT[:, 0:256], [[1, 256]], -128, -1)
            ts("vector", MSK[:, 256:512], IOT[:, 0:256], 0.0, None, ALU.is_ge)
            for c in range(33):
                p.dma("sync", CBC[:, c:c + 1], CONV_B[c])
            for c in range(32):
                p.dma("sync", NWC[:, c:c + 1], NORM_W[c])
            for c in range(16):
                p.dma("sync", BQC[:, c:c + 1], BQ[c])
                p.dma("sync", BOC[:, c:c + 1], BO[c])
                p.dma("sync", BBC[:, c:c + 1], B_BASE[c])
            for c in range(8):
                p.dma("sync", BKC[:, c:c + 1], BK[c])
                p.dma("sync", BVC[:, c:c + 1], BV[c])
            p.dma("sync", BRC[:, :], B_ROUTER)
            p.dma("sync", ROWS[:, 0:16], LUTV)
            mm(P[0][:, 0:16], ONE1[:, :], ROWS[:, 0:16], True, True)
            cp("vector", LUTB[:, :], P[0][:, 0:16])
            p.dma("sync", ROWS[:, :], DT_BIAS)
            mm(P[0][:, 0:32], ONE1[:, :], ROWS[:, :], True, True)
            cp("vector", DTBC[:, :], P[0][:, 0:32])
            p.dma("sync", ROWS[:, :], A_LOG)
            act("scalar", ROWS2[:, :], ROWS[:, :], AF.Exp)
            ts("vector", ROWS2[:, :], ROWS2[:, :], -1.0, None, ALU.mult)
            mm(P[0][:, 0:32], ONE1[:, :], ROWS2[:, :], True, True)
            cp("vector", ABC[:, :], P[0][:, 0:32])
            p.dma("sync", ROWS[:, :], DD)
            mm(P[0][:, 0:32], ONE1[:, :], ROWS[:, :], True, True)
            cp("vector", DBC[:, :], P[0][:, 0:32])
            p.dma("sync", FA[0:64, 0:128], RMAT)
            tr(P[0][:, 0:64], FA[0:64, 0:128], IDF[0:64, 0:64])
            cp("vector", RTB[:, :], P[0][:, 0:64])

        def ln_tile(src_f32_tile, out_bf_tile, sink=None, sb0=2048):
            if sink is None:
                sink = AR[:, 2048:4096]
            c0, c1, c2 = sb0, sb0 + 1, sb0 + 2
            p.op("vector", lambda e: e.reduce_sum(FA[:, c0:c0 + 1], src_f32_tile, axis=AXX),
                 ins=[src_f32_tile], outs=[FA[:, c0:c0 + 1]])
            ts("vector", FA[:, c0:c0 + 1], FA[:, c0:c0 + 1], float(1.0 / DM), None, ALU.mult)
            act("scalar", sink, src_f32_tile, AF.Square,
                accum_out=FA[:, c1:c1 + 1])
            ts("vector", FA[:, c1:c1 + 1], FA[:, c1:c1 + 1], float(1.0 / DM), None, ALU.mult)
            tt("vector", FA[:, c2:c2 + 1], FA[:, c0:c0 + 1], FA[:, c0:c0 + 1], ALU.mult)
            tt("vector", FA[:, c1:c1 + 1], FA[:, c1:c1 + 1], FA[:, c2:c2 + 1], ALU.subtract)
            ts("vector", FA[:, c1:c1 + 1], FA[:, c1:c1 + 1], EPS, None, ALU.add)
            act("scalar", FA[:, c1:c1 + 1], FA[:, c1:c1 + 1], AF.Ln)
            act("scalar", FA[:, c1:c1 + 1], FA[:, c1:c1 + 1], AF.Exp, scale=-0.5)
            ts("vector", out_bf_tile, src_f32_tile, FA[:, c0:c0 + 1],
               FA[:, c1:c1 + 1], ALU.subtract, ALU.mult)

        if "s1" in st_set:
            srcs1 = [FA[:, 0:2048], AR[:, 8192:12288].bitcast(F32)]
            lns1 = [AR[:, 0:2048], AR[:, 4096:6144]]
            sinks1 = [AR[:, 2048:4096], AR[:, 6144:8192]]
            for t in range(16):
                par = t % 2
                p.dma("sync", srcs1[par], U[t * 128:(t + 1) * 128, :])
                ln_tile(srcs1[par], lns1[par], sink=sinks1[par], sb0=2048 + 4 * par)
                p.dma("sync", XLNTM[t * 128:(t + 1) * 128, :], lns1[par])

        def load_xln_fm():
            for k in range(16):
                p.dmat(XLN[:, k * SEQ:(k + 1) * SEQ], XLNTM[:, k * 128:(k + 1) * 128])

        W_IN_R = W_IN.rearrange("(kc p) o -> p kc o", p=128)

        if "s2" in st_set:
            load_xln_fm()
            ARW_ = [AR[:, 2048:4096].rearrange("p (k o) -> p k o", k=16),
                    AR[:, 4096:6144].rearrange("p (k o) -> p k o", k=16)]
            STG_ = [AR[:, 6144:8192], AR[:, 8192:10240]]
            FAW = FA[:, 0:2048].rearrange("p (k o) -> p k o", k=16)
            for o in range(NOC_IN):
                ow = 128 if o < 65 else 32
                ARW = ARW_[o % 2]
                STG = STG_[o % 2]
                pb = 4 * (o % 2)
                p.dma("sync", FAW[:, :, 0:ow], W_IN_R[:, :, o * 128:o * 128 + ow])
                act("scalar", ARW[:, :, 0:ow], FAW[:, :, 0:ow], AF.Copy)
                for t in range(4):
                    for k in range(16):
                        mm(P[pb + t][0:ow, :], ARW[:, k, 0:ow],
                           XLN[:, k * SEQ + t * 512:k * SEQ + (t + 1) * 512],
                           start=(k == 0), stop=(k == 15))
                if o < 32:
                    for t in range(4):
                        act("scalar", STG[t * 512:(t + 1) * 512] if False else STG[:, t * 512:(t + 1) * 512], P[pb + t][:, :], AF.Silu)
                    p.dma("sync", SILUZ[o * 128:(o + 1) * 128, :], STG[:, 0:2048])
                elif o < 65:
                    for t in range(4):
                        cp("vector", STG[:, t * 512:(t + 1) * 512], P[pb + t][:, :])
                    p.dma("sync", XBCFM[(o - 32) * 128:(o - 31) * 128, :], STG[:, 0:2048])
                else:
                    for t in range(4):
                        cp("vector", DTRAW[:, t * 512:(t + 1) * 512], P[pb + t][0:32, :])

        gtasks = []
        if "s8" in st_set:
            WGT_ = [XLN[:, 0:5504], XLN[:, 6144:11648]]
            TMPG = XLN[:, 12288:17792]
            IXU_ = [XLN[:, 18432:24064].bitcast(U16), XLN[:, 18432:24064].bitcast(U16)]
            STG16_ = [XLN[:, 24576:25600].bitcast(U16), XLN[:, 25600:26624].bitcast(U16)]
            I32B_ = [XLN[:, 26624:28672].bitcast(I32), XLN[:, 28672:30720].bitcast(I32)]

            def _stage_task(IDXD, STAGED, r, width):
                tasks = []
                for ci, c0 in enumerate(range(0, width, 1024)):
                    cw = min(1024, width - c0)
                    ib = I32B_[ci % 2]
                    sg = STG16_[ci % 2]

                    def go(ib=ib, sg=sg, c0=c0, cw=cw):
                        p.dma("sync", ib[:, 0:cw],
                              IDXD[r * 128:(r + 1) * 128, c0:c0 + cw])
                        act("scalar", sg[:, 0:cw], ib[:, 0:cw], AF.Copy)
                        p.dma("sync", STAGED[r * 128:(r + 1) * 128, c0:c0 + cw],
                              sg[:, 0:cw])
                    tasks.append(go)
                return tasks

            def _gath_task(STAGED, OUTD, k, tw):
                iu = IXU_[k % 2]
                wg = WGT_[k % 2]
                tasks = []
                tasks.append(lambda: p.dmat(iu[:, 0:tw], STAGED[:, k * 128:(k + 1) * 128]))
                tasks.append(lambda: ts("vector", wg[:, 0:tw], iu[:, 0:tw], 0.0,
                                        LUTB[:, 0:1], ALU.is_equal, ALU.mult))

                def mk(j):
                    def go():
                        ts("vector", TMPG[:, 0:tw], iu[:, 0:tw], float(j),
                           LUTB[:, j:j + 1], ALU.is_equal, ALU.mult)
                        tt("vector", wg[:, 0:tw], wg[:, 0:tw], TMPG[:, 0:tw], ALU.add)
                    return go
                for j in range(1, 16):
                    tasks.append(mk(j))
                tasks.append(lambda: p.dma("sync", OUTD[k * 128:(k + 1) * 128, :],
                                           wg[:, 0:tw]))
                return tasks

            for (IDXD, STAGED, OUTD, nr, wid, nc_) in (
                    (GIDX, IXA, GW, 43, 2048, 16),
                    (UIDX, IXA, UW, 43, 2048, 16),
                    (DIDX, IXB, DW, 16, 5504, 43)):
                for r in range(nr):
                    gtasks.extend(_stage_task(IDXD, STAGED, r, wid))
                for k in range(nc_):
                    gtasks.extend(_gath_task(STAGED, OUTD, k, 5504 if nr == 43 else 2048))

        def pump(n):
            for _ in range(n):
                if gtasks:
                    gtasks.pop(0)()

        if "s3" in st_set:
            for c in range(33):
                xb = 2052 * (c % 2)
                sb_ = 8192 + 2048 * (c % 2)
                fb = 2052 + 4 * (c % 2)
                pb = 4 * (c % 2)
                p.dma("sync", FA[:, fb:fb + 4], CONV_W[c])
                memset("vector", AR[:, xb:xb + 3], 0.0)
                p.dma("sync", AR[:, xb + 3:xb + 2051], XBCFM[c * 128:(c + 1) * 128, :])
                dg = [WBF[:, k, :] for k in range(4)] if c % 2 == 0 else \
                     [SSD1[:, k * 128:(k + 1) * 128] for k in range(4)]
                for k in range(4):
                    ts("vector", dg[k], IDB[:, :], FA[:, fb + k:fb + k + 1],
                       None, ALU.mult)
                for t in range(4):
                    for k in range(4):
                        mm(P[pb + t][:, :], dg[k],
                           AR[:, xb + t * 512 + k:xb + t * 512 + k + 512],
                           start=(k == 0), stop=(k == 3))
                for t in range(4):
                    act("scalar", AR[:, sb_ + t * 512:sb_ + (t + 1) * 512],
                        P[pb + t][:, :], AF.Silu, bias=CBC[:, c:c + 1])
                p.dma("sync", XCFM[c * 128:(c + 1) * 128, :], AR[:, sb_:sb_ + 2048])
                pump(PUMP_S3)

        if "s4" in st_set:
            memset("vector", STATE[:, :], 0.0)
            for i in range(16):
                tr(P[5][:, 0:32], DTRAW[:, i * 128:(i + 1) * 128], IDF[0:32, 0:32])
                sl = slice(i * 32, (i + 1) * 32)
                tt("vector", FA[:, 0:32], P[5][:, 0:32], DTBC[:, :], ALU.add)
                act("scalar", FA[:, 32:64], FA[:, 0:32], AF.Abs)
                act("scalar", FA[:, 32:64], FA[:, 32:64], AF.Exp, scale=-1.0)
                act("scalar", FA[:, 32:64], FA[:, 32:64], AF.Ln, bias=1.0)
                act("scalar", FA[:, 64:96], FA[:, 0:32], AF.Relu)
                tt("vector", DTT[:, sl], FA[:, 32:64], FA[:, 64:96], ALU.add)
                act("scalar", LNDT[:, sl], DTT[:, sl], AF.Ln)
                tt("vector", ADTT[:, sl], DTT[:, sl], ABC[:, :], ALU.mult)
            for c in range(8):
                s0 = ADTT[:, (2 * c) * 32:(2 * c + 1) * 32]
                s1 = ADTT[:, (2 * c + 1) * 32:(2 * c + 2) * 32]
                mm(P[5][:, 0:32], TRI[:, :], s0, True, True)
                cp("vector", ACUM[:, (2 * c) * 32:(2 * c + 1) * 32], P[5][:, 0:32])
                mm(P[5][:, 32:64], ONEF[:, :], s0, True, False)
                mm(P[5][:, 32:64], TRI[:, :], s1, False, True)
                cp("vector", ACUM[:, (2 * c + 1) * 32:(2 * c + 2) * 32], P[5][:, 32:64])
            for i in range(16):
                tr(P[6][0:32, 0:128], ACUM[:, i * 32:(i + 1) * 32], IDF[:, :])
                cp("vector", ACFM[:, i * 128:(i + 1) * 128], P[6][0:32, 0:128])
                pump(PUMP_S4)
            p.dma("sync", CFMS[:, :], XCFM[4160:4224, :])
            p.dma("sync", BCF[0:64, :], XCFM[4096:4160, :])
            for i in range(16):
                p.dmat(BCT[:, i * 128:(i + 1) * 128],
                       XCFM[4096:4224, i * 128:(i + 1) * 128])

        if "s5" in st_set:
            iota(IOT[0:32, 0:128], [[0, 128]], 0, 1)
            XCR = XCFM[0:4096, :].rearrange("(hc p) t -> p hc t", p=128)
            SZR = SILUZ.rearrange("(hc p) t -> p hc t", p=128)
            YRR = YRMS.rearrange("(hc p) t -> p hc t", p=128)
            for c in range(8):
                csl = slice(c * 256, (c + 1) * 256)
                for st_i in range(2):
                    p.dmat(AR[:, st_i * 4096:(st_i + 1) * 4096],
                           XCFM[0:4096, c * 256 + st_i * 128:c * 256 + (st_i + 1) * 128])
                p.dma("sync", AR[:, 8192:16384].rearrange("p (hc t) -> p hc t", hc=32),
                      XCR[:, :, csl])
                p.dma("sync", AR[:, 16384:24576].rearrange("p (hc t) -> p hc t", hc=32),
                      SZR[:, :, csl])
                for st_i in range(2):
                    mm(P[1][:, st_i * 256:(st_i + 1) * 256],
                       BCF[0:64, c * 256 + st_i * 128:c * 256 + (st_i + 1) * 128],
                       CFMS[0:64, csl], True, True)
                    tt("vector", SSD1[:, st_i * 256:(st_i + 1) * 256],
                       P[1][:, st_i * 256:(st_i + 1) * 256],
                       MSK[:, st_i * 256:(st_i + 1) * 256], ALU.mult)
                for h in range(32):
                    par = h % 2
                    tx = [(2 * c) * 32 + h, (2 * c + 1) * 32 + h]
                    xtm = [AR[:, st_i * 4096 + h * 128:st_i * 4096 + (h + 1) * 128]
                           for st_i in range(2)]
                    Pb = P[0] if par == 0 else P[5]
                    Py = P[2] if par == 0 else P[6]
                    Ps = P[3] if par == 0 else P[7]
                    dif = FA[:, 0:512] if par == 0 else FA[:, 768:1280]
                    ybuf = FA[:, 512:768] if par == 0 else FA[:, 1280:1536]
                    dcol = FA[:, 2560 + 2 * par:2562 + 2 * par]
                    acol = FA[0:64, 2566 + par:2567 + par]
                    if par == 0:
                        Ebuf = SSD1[:, 512:1024]
                        Mbuf = SSD1[:, 1024:1536]
                        E64 = SSD1[0:64, 1536:1792]
                        CSc = SSD1[0:64, 1792:2048]
                        sqb = FA[:, 2304:2432].bitcast(BF16)
                        bdsl = 0
                    else:
                        Ebuf = FA[:, 1536:1792].bitcast(BF16)
                        Mbuf = FA[:, 1792:2048].bitcast(BF16)
                        E64 = FA[0:64, 2048:2176].bitcast(BF16)
                        CSc = FA[0:64, 2176:2304].bitcast(BF16)
                        sqb = FA[:, 2432:2560].bitcast(BF16)
                        bdsl = 1
                    selp = SELH[:, par * 128:(par + 1) * 128]
                    ts("vector", selp, IOT[0:32, 0:128], float(h), None, ALU.is_equal)
                    mm(Pb[:, 0:256], selp, ACFM[:, csl], True, True)
                    for st_i in range(2):
                        ts("vector", dif[:, st_i * 256:(st_i + 1) * 256], Pb[:, 0:256],
                           ACUM[:, tx[st_i]:tx[st_i] + 1], 0.0, ALU.subtract, ALU.min)
                        act("scalar", Ebuf[:, st_i * 256:(st_i + 1) * 256],
                            dif[:, st_i * 256:(st_i + 1) * 256],
                            AF.Exp, bias=LNDT[:, tx[st_i]:tx[st_i] + 1])
                        tt("vector", Mbuf[:, st_i * 256:(st_i + 1) * 256],
                           Ebuf[:, st_i * 256:(st_i + 1) * 256],
                           SSD1[:, st_i * 256:(st_i + 1) * 256], ALU.mult)
                    mm(Py[:, 0:256], xtm[0], Mbuf[:, 0:256], True, False)
                    mm(Py[:, 0:256], xtm[1], Mbuf[:, 256:512], False, False)
                    cp("vector", STATEB[:, par * 128:(par + 1) * 128],
                       STATE[:, h * 128:(h + 1) * 128])
                    act("scalar", E64[:, :], Pb[0:64, 0:256], AF.Exp)
                    tt("vector", CSc[:, :], E64[:, :], CFMS[0:64, csl], ALU.mult)
                    mm(Py[:, 0:256], STATEB[:, par * 128:(par + 1) * 128],
                       CSc[:, :], False, True)
                    for st_i in range(2):
                        ts("vector", dcol[:, st_i:st_i + 1], Pb[:, 255:256],
                           ACUM[:, tx[st_i]:tx[st_i] + 1],
                           LNDT[:, tx[st_i]:tx[st_i] + 1], ALU.subtract, ALU.add)
                        act("scalar", dcol[:, st_i:st_i + 1],
                            dcol[:, st_i:st_i + 1], AF.Exp)
                        ts("vector", WBF[:, bdsl, st_i * 64:(st_i + 1) * 64],
                           BCT[:, (2 * c + st_i) * 128:(2 * c + st_i) * 128 + 64],
                           dcol[:, st_i:st_i + 1], None, ALU.mult)
                        mm(Ps[0:64, 0:128], WBF[:, bdsl, st_i * 64:(st_i + 1) * 64],
                           xtm[st_i], start=(st_i == 0), stop=(st_i == 1))
                    stt("vector", ybuf[:, :], AR[:, 8192 + h * 256:8192 + (h + 1) * 256],
                        DBC[:, h:h + 1], Py[:, 0:256], ALU.mult, ALU.add)
                    tt("vector", AR[:, 24576 + h * 256:24576 + (h + 1) * 256],
                       ybuf[:, :], AR[:, 16384 + h * 256:16384 + (h + 1) * 256],
                       ALU.mult)
                    act("scalar", sqb[:, :],
                        AR[:, 24576 + h * 256:24576 + (h + 1) * 256], AF.Square)
                    mm(P[4][:, 0:256], ONEB[:, :], sqb[:, :],
                       start=(h == 0), stop=(h == 31))
                    act("scalar", acol[:, :], Pb[0:64, 255:256], AF.Exp)
                    stt("vector", STATE[:, h * 128:(h + 1) * 128],
                        STATE[:, h * 128:(h + 1) * 128], acol[:, :],
                        Ps[0:64, 0:128], ALU.mult, ALU.add)
                    pump(PUMP_H)
                ts("vector", FA[:, 0:256], P[4][:, 0:256], float(1.0 / DIN), EPS,
                   ALU.mult, ALU.add)
                act("scalar", FA[:, 0:256], FA[:, 0:256], AF.Ln)
                act("scalar", FA[:, 0:256], FA[:, 0:256], AF.Exp, scale=-0.5)
                for h in range(32):
                    stt("vector", AR[:, h * 256:(h + 1) * 256],
                        AR[:, 24576 + h * 256:24576 + (h + 1) * 256],
                        NWC[:, h:h + 1], FA[:, 0:256], ALU.mult, ALU.mult)
                p.dma("sync", YRR[:, :, csl],
                      AR[:, 0:8192].rearrange("p (hc t) -> p hc t", hc=32))
                pump(PUMP_S5)

        if "s6" in st_set:
            W_OUT_R = W_OUT.rearrange("(kc p) o -> p kc o", p=128)
            ARW2 = AR[:, 0:8192].rearrange("p (ko o) -> p ko o", o=128)
            for op_ in range(8):
                FAW6 = FA[:, 0:2048].rearrange("p (k o) -> p k o", k=16)
                for o2 in range(2):
                    o = op_ * 2 + o2
                    for hf in range(2):
                        p.dma("sync", FAW6[:, :, :],
                              W_OUT_R[:, hf * 16:(hf + 1) * 16, o * 128:(o + 1) * 128])
                        act("scalar", ARW2[:, o2 * 32 + hf * 16:o2 * 32 + (hf + 1) * 16, :],
                            FAW6[:, :, :], AF.Copy)
                for k in range(32):
                    p.dma("sync", AR[:, 16384:18432], YRMS[k * 128:(k + 1) * 128, :])
                    for o2 in range(2):
                        for t in range(4):
                            mm(P[o2 * 4 + t][:, :], ARW2[:, o2 * 32 + k, :],
                               AR[:, 16384 + t * 512:16384 + (t + 1) * 512],
                               start=(k == 0), stop=(k == 31))
                for o2 in range(2):
                    o = op_ * 2 + o2
                    for t in range(4):
                        cp("vector", AR[:, 18432 + t * 512:18432 + (t + 1) * 512],
                           P[o2 * 4 + t][:, :])
                    p.dma("sync", BOUT[o * 128:(o + 1) * 128, :], AR[:, 18432:20480])
                pump(PUMP_S6)

        def resid_ln(xold, xnew, write_ln):
            srcs = [FA[:, 0:2048], AR[:, 8192:12288].bitcast(F32)]
            bos = [AR[:, 0:2048], AR[:, 4096:6144]]
            sinks = [AR[:, 2048:4096], AR[:, 6144:8192]]
            lns = [AR[:, 12288:14336], AR[:, 14336:16384]]
            for t in range(16):
                par = t % 2
                p.dmat(bos[par], BOUT[:, t * 128:(t + 1) * 128])
                p.dma("sync", srcs[par], xold[t * 128:(t + 1) * 128, :])
                tt("vector", srcs[par], srcs[par], bos[par], ALU.add)
                p.dma("sync", xnew[t * 128:(t + 1) * 128, :], srcs[par])
                if write_ln:
                    ln_tile(srcs[par], lns[par], sink=sinks[par], sb0=2048 + 4 * par)
                    p.dma("sync", XLNTM[t * 128:(t + 1) * 128, :], lns[par])

        if "s7" in st_set:
            resid_ln(U, X1, True)

        if "s8" in st_set:
            pump(10 ** 9)
            load_xln_fm()
            GW_R = GW.rearrange("(kc p) o -> p kc o", p=128)
            UW_R = UW.rearrange("(kc p) o -> p kc o", p=128)
            ARG_ = [AR[:, 22016:24064].rearrange("p (k o) -> p k o", k=16),
                    AR[:, 24064:26112].rearrange("p (k o) -> p k o", k=16)]
            ARU_ = [AR[:, 26112:28160].rearrange("p (k o) -> p k o", k=16),
                    AR[:, 28160:30208].rearrange("p (k o) -> p k o", k=16)]
            SILU_ = [AR[:, 30208:30720], AR[:, 30720:31232]]
            HQ = AR[:, 0:22016]
            for q in range(4):
                qsl = lambda k: slice(k * SEQ + q * 512, k * SEQ + (q + 1) * 512)
                for o in range(43):
                    par = o % 2
                    ARG = ARG_[par]
                    ARU = ARU_[par]
                    pg = P[2 * par]
                    pu = P[1 + 2 * par]
                    p.dma("sync", ARG[:, :, :], GW_R[:, :, o * 128:(o + 1) * 128])
                    p.dma("sync", ARU[:, :, :], UW_R[:, :, o * 128:(o + 1) * 128])
                    for k in range(16):
                        mm(pg[:, :], ARG[:, k, :], XLN[:, qsl(k)],
                           start=(k == 0), stop=(k == 15))
                        mm(pu[:, :], ARU[:, k, :], XLN[:, qsl(k)],
                           start=(k == 0), stop=(k == 15))
                    act("scalar", SILU_[par][:, :], pg[:, :], AF.Silu)
                    tt("vector", HQ[:, o * 512:(o + 1) * 512], SILU_[par][:, :],
                       pu[:, :], ALU.mult)
                for dh in range(2):
                    for o in range(43):
                        dwb = AR[:, 26112 + 1024 * (o % 2):27136 + 1024 * (o % 2)]
                        p.dma("sync", dwb,
                              DW[o * 128:(o + 1) * 128,
                                 dh * 1024:(dh + 1) * 1024])
                        dwv = dwb.rearrange("p (dc d) -> p dc d", d=128)
                        for dc in range(8):
                            mm(P[dc][:, :], dwv[:, dc, :], HQ[:, o * 512:(o + 1) * 512],
                               start=(o == 0), stop=(o == 42))
                    for dc in range(8):
                        d = dh * 8 + dc
                        sg8 = AR[:, 28160 + 512 * (dc % 2):28672 + 512 * (dc % 2)]
                        cp("vector", sg8, P[dc][:, :])
                        p.dma("sync", BOUT[d * 128:(d + 1) * 128, q * 512:(q + 1) * 512],
                              sg8)

        if "s9" in st_set:
            resid_ln(X1, X2, True)


        if "s10" in st_set:
            load_xln_fm()
            COSR_ = AR[:, 12288:14336]
            SINS_ = AR[:, 14336:16384]
            iota(IOT[:, 0:1], [[0, 1]], 0, 1)
            cp("vector", SHC[:, :], IOT[:, 0:1])
            ts("vector", FA[:, 0:1], SHC[:, :], 64.0, -64.0,
               ALU.is_ge, ALU.mult)
            tt("vector", SHC[:, :], SHC[:, :], FA[:, 0:1], ALU.add)
            ts("vector", SHC[:, :], SHC[:, :],
               -float(np.log(1e4) / 64.0), None, ALU.mult)
            act("scalar", SHC[:, :], SHC[:, :], AF.Exp)
            for th in range(2):
                iota(IOT[:, 0:1024], [[1, 1024]], th * 1024, 0)
                cp("vector", FA[:, 0:1024], IOT[:, 0:1024])
                ts("vector", FA[:, 0:1024], FA[:, 0:1024], SHC[:, :],
                   None, ALU.mult)
                sl = slice(th * 1024, (th + 1) * 1024)
                for (dst, shift) in ((COSR_, 0.25), (SINS_, 0.0)):
                    ts("vector", FA[:, 1024:2048], FA[:, 0:1024],
                       float(1.0 / (2 * PI)), float(shift), ALU.mult, ALU.add)
                    cp("vector", IOT[:, 0:1024], FA[:, 1024:2048])
                    cp("vector", FA[:, 2048:3072], IOT[:, 0:1024])
                    tt("vector", FA[:, 1024:2048], FA[:, 1024:2048],
                       FA[:, 2048:3072], ALU.subtract)
                    ts("vector", FA[:, 1024:2048], FA[:, 1024:2048],
                       float(2 * PI), None, ALU.mult)
                    act("scalar", dst[:, sl], FA[:, 1024:2048], AF.Sin)
            ts("vector", SINS_[0:64, :], SINS_[0:64, :], -1.0, None, ALU.mult)

            RTBD = WBF[:, 2, :]
            cp("vector", RTBD[0:128, 0:64], RTB[:, :])
            cp("vector", RTBD[0:128, 64:128], RTB[:, :])
            memset("vector", SHC[:, :], 0.25)
            memset("vector", SHC[0:64, :], 0.0)

            def proj(Wdram, n_out, bias_cols, dst, vscale=None):
                for o in range(n_out):
                    p.dma("sync", FA[:, 0:2048], Wdram[o * 128:(o + 1) * 128, :])
                    for k in range(16):
                        ptr_ = P[5 + k % 2]
                        tr(ptr_[:, 0:128], FA[:, k * 128:(k + 1) * 128], IDF[:, :])
                        cp("vector", AR[:, k * 128:(k + 1) * 128], ptr_[:, 0:128])
                    for k in range(16):
                        for t in range(4):
                            mm(P[t][:, :], AR[:, k * 128:(k + 1) * 128],
                               XLN[:, k * SEQ + t * 512:k * SEQ + (t + 1) * 512],
                               start=(k == 0), stop=(k == 15))
                    for t in range(4):
                        if vscale is None:
                            ts("vector", AR[:, 2048 + t * 512:2048 + (t + 1) * 512],
                               P[t][:, :], bias_cols[:, o:o + 1], None, ALU.add)
                        else:
                            ts("vector", AR[:, 2048 + t * 512:2048 + (t + 1) * 512],
                               P[t][:, :], bias_cols[:, o:o + 1], vscale,
                               ALU.add, ALU.mult)
                    p.dma("sync", dst[o * 128:(o + 1) * 128, :], AR[:, 2048:4096])

            proj(WQ, 16, BQC, QFM)
            proj(WK, 8, BKC, KFM)
            proj(WV, 8, BVC, VSC, vscale=float(2.0 / 128.0))

            def feats_head(srcd, h, dstd):
                p.dma("sync", AR[:, 4096:6144], srcd[h * 128:(h + 1) * 128, :])
                p.dma("sync", AR[0:64, 6144:8192], AR[64:128, 4096:6144])
                p.dma("sync", AR[64:128, 6144:8192], AR[0:64, 4096:6144])
                tt("vector", AR[:, 8192:10240], AR[:, 4096:6144], COSR_, ALU.mult)
                tt("vector", AR[:, 10240:12288], AR[:, 6144:8192], SINS_, ALU.mult)
                tt("vector", AR[:, 4096:6144], AR[:, 8192:10240],
                   AR[:, 10240:12288], ALU.add)
                act("scalar", AR[:, 8192:10240], AR[:, 4096:6144], AF.Square)
                for t in range(4):
                    mm(P[t][:, :], ONEB[:, :], AR[:, 8192 + t * 512:8192 + (t + 1) * 512],
                       True, True)
                for t in range(4):
                    ts("vector", FA[:, 0:512], P[t][:, :], 1e-12, None, ALU.max)
                    act("scalar", FA[:, 0:512], FA[:, 0:512], AF.Ln)
                    act("scalar", FA[:, 0:512], FA[:, 0:512], AF.Exp, scale=-0.5)
                    mm(P[4][:, 0:512], RTBD[:, :],
                       AR[:, 4096 + t * 512:4096 + (t + 1) * 512], True, True)
                    tt("vector", FA[:, 512:1024], P[4][:, 0:512], FA[:, 0:512], ALU.mult)
                    ts("vector", FA[:, 1024:1536], FA[:, 512:1024],
                       float(1.0 / (2 * PI)), SHC[:, :], ALU.mult, ALU.add)
                    cp("vector", IOT[:, 0:512], FA[:, 1024:1536])
                    cp("vector", FA[:, 1536:2048], IOT[:, 0:512])
                    tt("vector", FA[:, 1024:1536], FA[:, 1024:1536],
                       FA[:, 1536:2048], ALU.subtract)
                    ts("vector", FA[:, 1024:1536], FA[:, 1024:1536],
                       float(2 * PI), None, ALU.mult)
                    act("scalar", AR[:, 10240 + t * 512:10240 + (t + 1) * 512],
                        FA[:, 1024:1536], AF.Sin)
                p.dma("sync", dstd[h * 128:(h + 1) * 128, :], AR[:, 10240:12288])

            for h in range(16):
                feats_head(QFM, h, QPFM)
            for h in range(8):
                feats_head(KFM, h, KPFM)

            for i in range(16):
                p.dmat(AR[:, 0:2048], QPFM[:, i * 128:(i + 1) * 128])
                p.dmat(AR[:, 2048:3072], KPFM[:, i * 128:(i + 1) * 128])
                p.dmat(AR[:, 3072:4096], VSC[:, i * 128:(i + 1) * 128])
                for h in range(16):
                    for j in range(8):
                        p.op("vector", lambda e, h=h, j=j: e.scalar_tensor_tensor(
                            AR[:, 4096:4224], AR[:, h * 128:(h + 1) * 128], 1.0,
                            AR[:, 2048 + j * 128:2048 + (j + 1) * 128],
                            op0=ALU.mult, op1=ALU.mult,
                            accum_out=FA[:, 2048 + h * 8 + j:2049 + h * 8 + j]),
                            ins=[AR[:, h * 128:(h + 1) * 128],
                                 AR[:, 2048 + j * 128:2048 + (j + 1) * 128]],
                            outs=[AR[:, 4096:4224],
                                  FA[:, 2048 + h * 8 + j:2049 + h * 8 + j]])
                for h in range(16):
                    acc = FA[:, h * 128:(h + 1) * 128]
                    ts("vector", acc, AR[:, 3072:3200],
                       FA[:, 2048 + h * 8:2049 + h * 8], None, ALU.mult)
                    for j in range(1, 8):
                        stt("vector", acc, AR[:, 3072 + j * 128:3072 + (j + 1) * 128],
                            FA[:, 2048 + h * 8 + j:2049 + h * 8 + j], acc,
                            ALU.mult, ALU.add)
                act("scalar", AR[:, 4096:6144], FA[:, 0:2048], AF.Copy)
                OSCR = OTM.rearrange("(h ii a) (b d) -> ii (a b) h d",
                                     h=16, ii=16, a=8, d=128)
                p.dma("sync", OSCR[i],
                      AR[:, 4096:6144].rearrange("p (h d) -> p h d", h=16))

            for k in range(16):
                p.dmat(XLN[:, k * SEQ:(k + 1) * SEQ], OTM[:, k * 128:(k + 1) * 128])
            proj(WO, 16, BOC, BOUT)

        if "s11" in st_set:
            resid_ln(X2, X3, True)

        if "s12" in st_set:
            load_xln_fm()
            ASB = AR[:, 0:256].rearrange("p (k r) -> p k r", k=16)
            p.dma("sync", FA[:, 2048:2304].rearrange("p (k r) -> p k r", k=16),
                  A_S.rearrange("(kc p) r -> p kc r", p=128))
            act("scalar", ASB[:, :, :], FA[:, 2048:2304].rearrange("p (k r) -> p k r", k=16), AF.Copy)
            WRB = AR[:, 256:352].rearrange("p (k r) -> p k r", k=16)
            p.dma("sync", FA[:, 2304:2400].rearrange("p (k r) -> p k r", k=16),
                  W_ROUTER.rearrange("(kc p) r -> p kc r", p=128))
            act("scalar", WRB[:, :, :], FA[:, 2304:2400].rearrange("p (k r) -> p k r", k=16), AF.Copy)
            ARB = AR[:, 512:2048].rearrange("p (e k r) -> p e k r", e=6, k=16)
            for e6 in range(6):
                p.dma("sync", FA[:, 2400:2656].rearrange("p (k r) -> p k r", k=16),
                      A_R.rearrange("e (kc p) r -> p e kc r", p=128)[:, e6])
                act("scalar", ARB[:, e6, :, :],
                    FA[:, 2400:2656].rearrange("p (k r) -> p k r", k=16), AF.Copy)
            BSB = AR[0:16, 2048:4096]
            p.dma("sync", FA[0:16, 0:2048], B_S)
            act("scalar", BSB, FA[0:16, 0:2048], AF.Copy)
            BRB = AR[0:16, 4096:16384].rearrange("p (e d) -> p e d", e=6)
            for e6 in range(6):
                p.dma("sync", FA[0:16, 0:2048], B_R[e6])
                act("scalar", BRB[:, e6, :], FA[0:16, 0:2048], AF.Copy)
            for t in range(4):
                for k in range(16):
                    mm(P[4 + t][0:6, :], WRB[:, k, :],
                       XLN[:, k * SEQ + t * 512:k * SEQ + (t + 1) * 512],
                       start=(k == 0), stop=(k == 15))
            for t in range(4):
                ts("vector", FA[0:6, t * 512:(t + 1) * 512], P[4 + t][0:6, :],
                   BRC[:, :], None, ALU.add)
            for i in range(16):
                isl = slice(i * 128, (i + 1) * 128)
                tr(P[0][:, 0:6], FA[0:6, isl], IDF[0:6, 0:6])
                cp("vector", FA[:, 2048:2054], P[0][:, 0:6])
                p.op("vector", lambda e: e.reduce_max(FA[:, 2054:2055], FA[:, 2048:2054], axis=AXX),
                     ins=[FA[:, 2048:2054]], outs=[FA[:, 2054:2055]])
                ts("vector", FA[:, 2048:2054], FA[:, 2048:2054], FA[:, 2054:2055],
                   None, ALU.subtract)
                act("scalar", FA[:, 2048:2054], FA[:, 2048:2054], AF.Exp)
                p.op("vector", lambda e: e.reduce_sum(FA[:, 2055:2056], FA[:, 2048:2054], axis=AXX),
                     ins=[FA[:, 2048:2054]], outs=[FA[:, 2055:2056]])
                p.op("vector", lambda e: e.reciprocal(FA[:, 2056:2057], FA[:, 2055:2056]),
                     ins=[FA[:, 2055:2056]], outs=[FA[:, 2056:2057]])
                ts("vector", FA[:, 2048:2054], FA[:, 2048:2054], FA[:, 2056:2057],
                   None, ALU.mult)
                p.op("vector", lambda e: e.reduce_max(FA[:, 2054:2055], FA[:, 2048:2054], axis=AXX),
                     ins=[FA[:, 2048:2054]], outs=[FA[:, 2054:2055]])
                ts("vector", FA[:, 2057:2063], FA[:, 2048:2054], FA[:, 2054:2055],
                   -1.0, ALU.is_equal, ALU.mult)
                ts("vector", FA[:, 2057:2063], FA[:, 2057:2063], 1.0, None, ALU.add)
                tt("vector", FA[:, 2057:2063], FA[:, 2057:2063], FA[:, 2048:2054], ALU.mult)
                p.op("vector", lambda e: e.reduce_max(FA[:, 2054:2055], FA[:, 2057:2063], axis=AXX),
                     ins=[FA[:, 2057:2063]], outs=[FA[:, 2054:2055]])
                ts("vector", FA[:, 2057:2063], FA[:, 2048:2054], FA[:, 2054:2055],
                   None, ALU.is_ge)
                tt("vector", FA[:, 2048:2054], FA[:, 2048:2054], FA[:, 2057:2063], ALU.mult)
                tr(P[0][0:6, 0:128], FA[:, 2048:2054], IDF[:, :])
                cp("vector", FA[0:6, isl], P[0][0:6, 0:128])
            for t in range(4):
                for k in range(16):
                    mm(P[4 + t][0:16, :], ASB[:, k, :],
                       XLN[:, k * SEQ + t * 512:k * SEQ + (t + 1) * 512],
                       start=(k == 0), stop=(k == 15))
            H16 = AR[0:16, 16384:18432]
            for t in range(4):
                cp("vector", H16[:, t * 512:(t + 1) * 512], P[4 + t][0:16, :])
            ART = AR[0:16, 18432:18944]
            for e6 in range(6):
                for t in range(4):
                    for k in range(16):
                        mm(P[0][0:16, :], ARB[:, e6, k, :],
                           XLN[:, k * SEQ + t * 512:k * SEQ + (t + 1) * 512],
                           start=(k == 0), stop=(k == 15))
                    mm(P[1][0:16, :], SEL6[:, e6 * 16:(e6 + 1) * 16],
                       FA[0:6, t * 512:(t + 1) * 512], True, True)
                    cp("vector", ART[:, 0:512], P[0][0:16, :])
                    tt("vector", AR[0:16, 18944 + e6 * 2048 + t * 512:
                                 18944 + e6 * 2048 + (t + 1) * 512],
                       ART[:, 0:512], P[1][0:16, :], ALU.mult)
            W_BASE_R = W_BASE.rearrange("(kc p) o -> p kc o", p=128)
            WBV = SSD1[:, 0:2048].rearrange("p (k o) -> p k o", k=16)
            STG12 = FA[:, 1024:2048].bitcast(BF16)
            FAW12 = FA[:, 0:1024].rearrange("p (k o) -> p k o", k=8)
            for o in range(16):
                for hf in range(2):
                    p.dma("sync", FAW12[:, :, :],
                          W_BASE_R[:, hf * 8:(hf + 1) * 8, o * 128:(o + 1) * 128])
                    act("scalar", WBV[:, hf * 8:(hf + 1) * 8, :], FAW12[:, :, :], AF.Copy)
                for t in range(4):
                    tsl = slice(t * 512, (t + 1) * 512)
                    for k in range(16):
                        mm(P[t][:, :], WBV[:, k, :],
                           XLN[:, k * SEQ + t * 512:k * SEQ + (t + 1) * 512],
                           start=(k == 0), stop=False)
                    mm(P[t][:, :], BSB[:, o * 128:(o + 1) * 128], H16[:, tsl],
                       start=False, stop=False)
                    for e6 in range(6):
                        mm(P[t][:, :], BRB[:, e6, o * 128:(o + 1) * 128],
                           AR[0:16, 18944 + e6 * 2048 + t * 512:
                              18944 + e6 * 2048 + (t + 1) * 512],
                           start=False, stop=(e6 == 5))
                    ts("vector", STG12[:, tsl], P[t][:, :], BBC[:, o:o + 1],
                       None, ALU.add)
                p.dma("sync", BOUT[o * 128:(o + 1) * 128, :], STG12[:, 0:2048])

        if "s13" in st_set:
            resid_ln(X3, Y, False)

        p.emit(block, nc, sems)

    return nc


def _core_inputs(inputs, core):
    g = lambda k: np.ascontiguousarray(inputs[k])
    m = {
        "u": np.ascontiguousarray(inputs["L0_attn_u"][core]),
        "w_in": g("L0_attn_W_in"),
        "conv_w": g("L0_attn_conv_w").reshape(33, 128, 4),
        "conv_b": g("L0_attn_conv_b").reshape(33, 128, 1),
        "dt_bias": g("L0_attn_dt_bias").reshape(1, NH),
        "a_log": g("L0_attn_A_log").reshape(1, NH),
        "dd": g("L0_attn_D").reshape(1, NH),
        "norm_w": g("L0_attn_norm_w").reshape(32, 128, 1),
        "w_out": g("L0_attn_W_out"),
        "lutv": g("L0_ffn_lut").reshape(1, 16),
        "gidx": g("L0_ffn_gate_idx"),
        "uidx": g("L0_ffn_up_idx"),
        "didx": g("L0_ffn_down_idx"),
        "wq": g("L1_attn_Wq"),
        "wk": g("L1_attn_Wk"),
        "wv": g("L1_attn_Wv"),
        "wo": g("L1_attn_Wo"),
        "bq": g("L1_attn_bq").reshape(16, 128, 1),
        "bk": g("L1_attn_bk").reshape(8, 128, 1),
        "bv": g("L1_attn_bv").reshape(8, 128, 1),
        "bo": g("L1_attn_bo").reshape(16, 128, 1),
        "rmat": g("L1_attn_rand_matrix"),
        "w_base": g("L1_ffn_W_base"),
        "b_base": g("L1_ffn_b_base").reshape(16, 128, 1),
        "a_s": g("L1_ffn_A_s"),
        "b_s": g("L1_ffn_B_s"),
        "a_r": g("L1_ffn_A_r"),
        "b_r": g("L1_ffn_B_r"),
        "w_router": g("L1_ffn_W_router"),
        "b_router": g("L1_ffn_b_router").reshape(6, 1),
    }
    return m


def kernel(**inputs):
    nc = build_program()
    in_maps = [_core_inputs(inputs, c) for c in range(8)]
    res = run_bass_kernel_spmd(nc, in_maps, core_ids=list(range(8)))
    return np.stack([r["y"] for r in res.results], axis=0)
